# version 13
# baseline (speedup 1.0000x reference)
import numpy as np
from contextlib import ExitStack

import concourse.bass as bass
import concourse.mybir as mybir
from concourse.bass_utils import run_bass_kernel_spmd

F32 = mybir.dt.float32
BF16 = mybir.dt.bfloat16
AF = mybir.ActivationFunctionType
ALU = mybir.AluOpType
AX = mybir.AxisListType

P = 128
D = 2048
S = 2048
TOWN = 1024
NKV = 4
NH = 16
INTER = 5632
NI = 44
KC = 16
SCALE = float(D // NH) ** -0.5
MLIST = [2, 4, 6, 8, 10, 12, 14, 16]
QT_PAR = [[0, 3, 4, 7, 8, 11, 12, 15],
          [1, 2, 5, 6, 9, 10, 13, 14]]
NEG = -1.0e30
ARENA_W = 48000

ENGINES = ("sync", "gpsimd", "vector", "scalar", "tensor")


class Buf:
    __slots__ = ("name", "off", "words", "init_deps")

    def __init__(self, name, off=-1, words=0):
        self.name = name
        self.off = off
        self.words = words
        self.init_deps = None


class Op:
    __slots__ = ("idx", "eng", "emit", "deps", "kind", "dma_sem", "dma_val",
                 "needs_inc", "inc_val", "clock")

    def __init__(self, idx, eng, emit, kind, dma_sem=None, dma_val=0):
        self.idx = idx
        self.eng = eng
        self.emit = emit
        self.deps = {}
        self.kind = kind
        self.dma_sem = dma_sem
        self.dma_val = dma_val
        self.needs_inc = False
        self.inc_val = 0
        self.clock = None


class Builder:
    def __init__(self):
        self.ops = []
        self.state = {}
        self.dma_counts = {}

    def _st(self, b):
        s = self.state.get(id(b))
        if s is None:
            s = [None, [], b]
            self.state[id(b)] = s
        return s

    def op(self, eng, emit, r=(), w=(), dma=None):
        idx = len(self.ops)
        if dma is not None:
            cnt = self.dma_counts.get(dma, 0) + 1
            self.dma_counts[dma] = cnt
            o = Op(idx, eng, emit, "d", dma, 16 * cnt)
        else:
            o = Op(idx, eng, emit, "c")
        deps = o.deps

        def add(d, typ):
            if d is None:
                return
            old = deps.get(d)
            if old is None or (old != "raw" and typ == "raw"):
                deps[d] = typ

        for b in r:
            st = self._st(b)
            if b.init_deps:
                for d in b.init_deps:
                    add(d, "raw")
                b.init_deps = None
            add(st[0], "raw")
            st[1].append(idx)
        for b in w:
            st = self._st(b)
            if b.init_deps:
                for d in b.init_deps:
                    add(d, "waw")
                b.init_deps = None
            add(st[0], "waw")
            for d in st[1]:
                add(d, "war")
            st[0] = idx
            st[1] = []
        self.ops.append(o)
        return idx

    def events_of(self, b):
        s = self.state.get(id(b))
        if s is None:
            return []
        ev = list(s[1])
        if s[0] is not None:
            ev.append(s[0])
        return ev

    def finalize(self):
        ops = self.ops
        for x in ops:
            kept = {}
            for d, typ in x.deps.items():
                o = ops[d]
                if o.kind == "c" and x.kind == "c" and o.eng == x.eng:
                    if o.eng == "tensor":
                        continue
                    if typ != "raw":
                        continue
                kept[d] = typ
                if o.kind == "c":
                    o.needs_inc = True
            x.deps = kept
        cnt = {e: 0 for e in ENGINES}
        for o in ops:
            if o.kind == "c":
                if o.needs_inc:
                    cnt[o.eng] += 1
                o.inc_val = cnt[o.eng]

    def emit_stream(self, eng_name, eng, sems):
        know = {}
        own = "E_" + eng_name
        for o in self.ops:
            if o.eng != eng_name:
                continue
            need = {}
            for d in o.deps:
                p = self.ops[d]
                if p.kind == "d":
                    sem, val = p.dma_sem, p.dma_val
                else:
                    sem, val = "E_" + p.eng, p.inc_val
                    if sem == own and p.eng == eng_name and False:
                        pass
                if know.get(sem, 0) < val:
                    if need.get(sem, 0) < val:
                        need[sem] = val
                pc = p.clock
                if pc:
                    for s2, v2 in pc.items():
                        if know.get(s2, 0) < v2:
                            know[s2] = v2
                if know.get(sem, 0) < val:
                    know[sem] = val
            for sem, val in need.items():
                eng.wait_ge(sems[sem], val)
            inst = o.emit(eng)
            if o.kind == "d":
                inst.then_inc(sems[o.dma_sem], 16)
                o.clock = dict(know)
            elif o.needs_inc:
                inst.then_inc(sems[own], 1)
                o.clock = dict(know)


class Arena:
    def __init__(self, ap, words, bld):
        self.ap = ap
        self.bld = bld
        self.blocks = [(0, words, ())]

    def alloc(self, name, words):
        words = (words + 1) & ~1
        for i, (st, sz, gh) in enumerate(self.blocks):
            if sz >= words:
                b = Buf(name, st, words)
                b.init_deps = list(gh) if gh else None
                if sz == words:
                    self.blocks.pop(i)
                else:
                    self.blocks[i] = (st + words, sz - words, gh)
                return b
        raise RuntimeError(f"arena OOM for {name} ({words} words); free={self.blocks}")

    def f32(self, b, c0=0, cn=None):
        cn = b.words if cn is None else cn
        return self.ap[:, b.off + c0 : b.off + c0 + cn]

    def bf(self, b, c0=0, cn=None):
        full = self.ap[:, b.off : b.off + b.words].bitcast(BF16)
        cn = 2 * b.words if cn is None else cn
        return full[:, c0 : c0 + cn]

    def free(self, b):
        ev = tuple(self.bld.events_of(b))
        blocks = self.blocks
        st, sz = b.off, b.words
        pos = 0
        while pos < len(blocks) and blocks[pos][0] < st:
            pos += 1
        blocks.insert(pos, (st, sz, ev))
        merged = []
        for blk in blocks:
            if merged and merged[-1][0] + merged[-1][1] == blk[0]:
                a = merged[-1]
                merged[-1] = (a[0], a[1] + blk[1], tuple(set(a[2]) | set(blk[2])))
            else:
                merged.append(blk)
        self.blocks = merged


def build_program(debug=()):
    nc = bass.Bass("TRN2", target_bir_lowering=False)

    x_full = nc.dram_tensor("x_full", [S, D], F32, kind="ExternalInput").ap()
    x_own = nc.dram_tensor("x_own", [TOWN, D], F32, kind="ExternalInput").ap()
    wq_h = nc.dram_tensor("wq", [KC, P, D], F32, kind="ExternalInput").ap()
    wk_h = nc.dram_tensor("wk", [P, KC * 512], F32, kind="ExternalInput").ap()
    wv_h = nc.dram_tensor("wv", [P, KC * 512], F32, kind="ExternalInput").ap()
    wo_h = nc.dram_tensor("wo", [4, P, KC * 512], F32, kind="ExternalInput").ap()
    gwT_h = nc.dram_tensor("gwT", [NI, P, KC * 128], F32, kind="ExternalInput").ap()
    upT_h = nc.dram_tensor("upT", [NI, P, KC * 128], F32, kind="ExternalInput").ap()
    dwT_h = nc.dram_tensor("dwT", [INTER, D], F32, kind="ExternalInput").ap()
    gs_h = nc.dram_tensor("gs", [INTER], F32, kind="ExternalInput").ap()
    us_h = nc.dram_tensor("us", [INTER], F32, kind="ExternalInput").ap()
    ds_h = nc.dram_tensor("ds", [D], F32, kind="ExternalInput").ap()
    qm_h = nc.dram_tensor("qm", [8, P, 256], F32, kind="ExternalInput").ap()
    y_h = nc.dram_tensor("y", [TOWN, D], F32, kind="ExternalOutput").ap()

    dbg_h = {}
    for nm, shp in debug:
        dbg_h[nm] = nc.dram_tensor("dbg_" + nm, list(shp), F32, kind="ExternalOutput").ap()

    bld = Builder()
    es = ExitStack()

    arena_t = es.enter_context(nc.sbuf_tensor("arena", [P, ARENA_W], F32))
    A = Arena(arena_t.ap(), ARENA_W, bld)

    psfp = [es.enter_context(nc.psum_tensor(f"psfp{i}", [P, 1024], F32)) for i in range(2)]
    psf45 = [es.enter_context(nc.psum_tensor(f"psf{i}", [P, 512], F32)) for i in range(4, 6)]
    psb = [es.enter_context(nc.psum_tensor(f"psb{i}", [P, 1024], BF16)) for i in range(2)]
    psf_b = [Buf(f"psf{i}") for i in range(6)]
    psb_b = [Buf(f"psb{i}") for i in range(2)]
    psfp_ap = [t.ap() for t in psfp]
    psf_ap = [psfp_ap[0][:, 0:512], psfp_ap[0][:, 512:1024],
              psfp_ap[1][:, 0:512], psfp_ap[1][:, 512:1024],
              psf45[0].ap(), psf45[1].ap()]
    psb_ap = [t.ap() for t in psb]

    sem_names = (["E_gpsimd", "E_vector", "E_scalar", "E_tensor", "E_sync"]
                 + ["dinit", "dwkv", "dxf0", "dxf1", "dwq0", "dwq1",
                    "dwo0", "dwo1", "dgw0", "dgw1", "dgw2", "dup0", "dup1", "dup2",
                    "ddw0", "ddw1", "ddw2", "ddw3", "dxs0", "dxs1", "dxs2", "dxs3", "dout"])
    sems = {nm: es.enter_context(nc.semaphore(nm)) for nm in sem_names}

    identb = A.alloc("identb", 64)
    identf = A.alloc("identf", 128)
    onesf = A.alloc("onesf", 128)
    zcol = A.alloc("zcol", 2)
    epscol = A.alloc("epscol", 2)
    gs_c = A.alloc("gs_c", 128)
    gs_sb = A.alloc("gs_sb", 44)
    us_c = A.alloc("us_c", 128)
    us_sb = A.alloc("us_sb", 44)
    qmask = A.alloc("qmask", 2048)
    stv = [A.alloc(f"st{i}", 24) for i in range(2)]
    mvv = [A.alloc(f"mv{i}", 2) for i in range(2)]
    lnvv = [A.alloc(f"lnv{i}", 2) for i in range(2)]
    rstdv = [A.alloc(f"rstd{i}", 2) for i in range(2)]
    negm = [A.alloc(f"negm{i}", 2) for i in range(8)]
    ssum = [A.alloc(f"ssum{i}", 2) for i in range(8)]
    invs = [A.alloc(f"invs{i}", 2) for i in range(8)]

    OP = bld.op

    OP("gpsimd", lambda g: g.memset(A.bf(identb), 0.0), w=[identb])
    OP("gpsimd", lambda g: g.affine_select(
        out=A.bf(identb), in_=A.bf(identb), compare_op=ALU.not_equal,
        fill=1.0, base=0, pattern=[[-1, P]], channel_multiplier=1),
        r=[identb], w=[identb])
    OP("gpsimd", lambda g: g.memset(A.f32(identf), 0.0), w=[identf])
    OP("gpsimd", lambda g: g.affine_select(
        out=A.f32(identf), in_=A.f32(identf), compare_op=ALU.not_equal,
        fill=1.0, base=0, pattern=[[-1, P]], channel_multiplier=1),
        r=[identf], w=[identf])
    OP("vector", lambda v: v.memset(A.f32(onesf)[0:1, :], 1.0), w=[onesf])
    OP("vector", lambda v: v.memset(A.f32(zcol, 0, 1), 0.0), w=[zcol])
    OP("vector", lambda v: v.memset(A.f32(epscol, 0, 1), 1.0e-5), w=[epscol])

    OP("gpsimd", lambda g: g.dma_start(
        out=A.f32(gs_c)[0:44, :],
        in_=gs_h.rearrange("(c p) -> c p", p=P)), w=[gs_c], dma="dinit")
    OP("gpsimd", lambda g: g.dma_start(
        out=A.f32(us_c)[0:44, :],
        in_=us_h.rearrange("(c p) -> c p", p=P)), w=[us_c], dma="dinit")
    OP("gpsimd", lambda g: g.dma_start(
        out=A.f32(qmask).rearrange("p (s f) -> p s f", f=256),
        in_=qm_h.rearrange("s p f -> p s f")), w=[qmask], dma="dinit")

    OP("tensor", lambda t: t.transpose(psf_ap[0][:, 0:44], A.f32(gs_c)[0:44, :],
                                       A.f32(identf)[0:44, 0:44]),
       r=[gs_c, identf], w=[psf_b[0]])
    OP("vector", lambda v: v.tensor_copy(A.f32(gs_sb), psf_ap[0][:, 0:44]),
       r=[psf_b[0]], w=[gs_sb])
    OP("tensor", lambda t: t.transpose(psf_ap[1][:, 0:44], A.f32(us_c)[0:44, :],
                                       A.f32(identf)[0:44, 0:44]),
       r=[us_c, identf], w=[psf_b[1]])
    OP("vector", lambda v: v.tensor_copy(A.f32(us_sb), psf_ap[1][:, 0:44]),
       r=[psf_b[1]], w=[us_sb])

    A.free(gs_c)
    A.free(us_c)

    def emit_ln(x_ap_fn, xbuf, out_tm_fn, outbuf, par):
        st, mv, lnv, rstd = stv[par], mvv[par], lnvv[par], rstdv[par]
        for i in range(4):
            OP("vector", lambda v, i=i: v.bn_stats(
                A.f32(st, 6 * i, 6), x_ap_fn()[:, 512 * i:512 * (i + 1)]),
                r=[xbuf], w=[st])
        OP("vector", lambda v: v.bn_aggr(A.f32(mv, 0, 2), A.f32(st, 0, 24)),
           r=[st], w=[mv])
        OP("scalar", lambda s: s.activation(
            A.f32(lnv, 0, 1), A.f32(mv, 1, 1), AF.Ln,
            bias=A.f32(epscol, 0, 1), scale=1.0), r=[mv, epscol], w=[lnv])
        OP("scalar", lambda s: s.activation(
            A.f32(rstd, 0, 1), A.f32(lnv, 0, 1), AF.Exp,
            bias=A.f32(zcol, 0, 1), scale=-0.5), r=[lnv, zcol], w=[rstd])
        OP("vector", lambda v: v.tensor_scalar(
            out_tm_fn(), x_ap_fn(), A.f32(mv, 0, 1), A.f32(rstd, 0, 1),
            ALU.subtract, ALU.mult), r=[xbuf, mv, rstd], w=[outbuf])

    def emit_tm_to_fm(tm_bufs, tm_fn, dst_buf, dst_fn, ntile, bank_off=0):
        for k in range(KC):
            bk = (k + bank_off) % 2
            for tt in range(ntile):
                OP("tensor", lambda t, k=k, tt=tt, bk=bk: t.transpose(
                    psb_ap[bk][:, 128 * tt:128 * (tt + 1)],
                    tm_fn(tt)[:, 128 * k:128 * (k + 1)], A.bf(identb)),
                    r=[tm_bufs[tt], identb], w=[psb_b[bk]])
            eng = "vector" if (k % 2 == 0) else "scalar"
            if eng == "vector":
                OP("vector", lambda v, k=k, bk=bk: v.tensor_copy(
                    dst_fn(k), psb_ap[bk][:, 0:128 * ntile]),
                    r=[psb_b[bk]], w=[dst_buf])
            else:
                OP("scalar", lambda s, k=k, bk=bk: s.copy(
                    dst_fn(k), psb_ap[bk][:, 0:128 * ntile]),
                    r=[psb_b[bk]], w=[dst_buf])

    xt = [A.alloc(f"xt{i}", 2048) for i in range(2)]
    ln1tm = [A.alloc(f"ln1tm{i}", 1024) for i in range(4)]
    ln1own = A.alloc("ln1own", 8192)
    q_fm = A.alloc("q_fm", 8192)
    k_fm = A.alloc("k_fm", 4096)
    v_tm = A.alloc("v_tm", 4096)

    wqb = [A.alloc(f"wqb{i}", 1024) for i in range(2)]
    for tg in range(2):
        for tl in range(4):
            tt = 4 * tg + tl
            xb = xt[tt % 2]
            OP("sync", lambda e, tt=tt, xb=xb: e.dma_start(
                out=A.f32(xb), in_=x_own[128 * tt:128 * (tt + 1), :]),
                w=[xb], dma=f"dxf{tt % 2}")
            emit_ln(lambda xb=xb: A.f32(xb), xb,
                    lambda tl=tl: A.bf(ln1tm[tl]), ln1tm[tl], tt % 2)
        emit_tm_to_fm(ln1tm, lambda tl: A.bf(ln1tm[tl]), ln1own,
                      lambda k, tg=tg: A.bf(ln1own, 1024 * k + 512 * tg, 512), 4)
        for f in range(KC):
            wb = wqb[f % 2]
            OP("gpsimd", lambda g, f=f, wb=wb: g.dma_start(
                out=A.bf(wb), in_=wq_h[f, :, :]), w=[wb], dma=f"dwq{f % 2}")
            pb = psf_b[(2 * f + tg) % 4]
            pa = psf_ap[(2 * f + tg) % 4]
            for k in range(KC):
                OP("tensor", lambda t, k=k, f=f, tg=tg, wb=wb, pa=pa: t.matmul(
                    pa[:, 0:512], A.bf(wb, 128 * k, 128),
                    A.bf(ln1own, 1024 * k + 512 * tg, 512),
                    start=(k == 0), stop=(k == KC - 1)),
                    r=[wb, ln1own], w=[pb])
            eng = "vector" if f % 2 == 0 else "scalar"
            dst = lambda f=f, tg=tg: A.bf(q_fm, 1024 * f + 512 * tg, 512)
            if eng == "vector":
                OP("vector", lambda v, pa=pa, dst=dst, pb=pb: v.tensor_copy(dst(), pa[:, 0:512]),
                   r=[pb], w=[q_fm])
            else:
                OP("scalar", lambda sx, pa=pa, dst=dst, pb=pb: sx.copy(dst(), pa[:, 0:512]),
                   r=[pb], w=[q_fm])

    A.free(ln1own)
    for b in wqb:
        A.free(b)

    wkb = A.alloc("wkb", 4096)
    wvb = A.alloc("wvb", 4096)
    OP("gpsimd", lambda g: g.dma_start(out=A.bf(wkb), in_=wk_h[:, :]),
       w=[wkb], dma="dwkv")
    OP("gpsimd", lambda g: g.dma_start(out=A.bf(wvb), in_=wv_h[:, :]),
       w=[wvb], dma="dwkv")

    ln1grp = A.alloc("ln1grp", 4096)
    for tg in range(4):
        for tl in range(4):
            tt = 4 * tg + tl
            xb = xt[tt % 2]
            OP("sync", lambda e, tt=tt, xb=xb: e.dma_start(
                out=A.f32(xb), in_=x_full[128 * tt:128 * (tt + 1), :]),
                w=[xb], dma=f"dxf{tt % 2}")
            emit_ln(lambda xb=xb: A.f32(xb), xb,
                    lambda tl=tl: A.bf(ln1tm[tl]), ln1tm[tl], tt % 2)
        emit_tm_to_fm(ln1tm, lambda tl: A.bf(ln1tm[tl]), ln1grp,
                      lambda k: A.bf(ln1grp, 512 * k, 512), 4)
        for j in range(NKV):
            pb = psf_b[j % 2]
            pa = psf_ap[j % 2]
            for k in range(KC):
                OP("tensor", lambda t, j=j, k=k, pa=pa: t.matmul(
                    pa[:, 0:512], A.bf(wkb, 512 * k + 128 * j, 128),
                    A.bf(ln1grp, 512 * k, 512),
                    start=(k == 0), stop=(k == KC - 1)), r=[wkb, ln1grp], w=[pb])
            eng = "vector" if j % 2 == 0 else "scalar"
            dst = lambda j=j, tg=tg: A.bf(k_fm, 2048 * j + 512 * tg, 512)
            if eng == "vector":
                OP("vector", lambda v, pa=pa, dst=dst: v.tensor_copy(dst(), pa[:, 0:512]),
                   r=[pb], w=[k_fm])
            else:
                OP("scalar", lambda s, pa=pa, dst=dst: s.copy(dst(), pa[:, 0:512]),
                   r=[pb], w=[k_fm])
        for sl in range(4):
            st_g = 4 * tg + sl
            pb = psf_b[2 + sl % 2]
            pa = psf_ap[2 + sl % 2]
            for k in range(KC):
                OP("tensor", lambda t, sl=sl, k=k, pa=pa: t.matmul(
                    pa[:, 0:512], A.bf(ln1grp, 512 * k + 128 * sl, 128),
                    A.bf(wvb, 512 * k, 512),
                    start=(k == 0), stop=(k == KC - 1)), r=[wvb, ln1grp], w=[pb])
            eng = "vector" if sl % 2 == 0 else "scalar"
            dst = lambda st_g=st_g: A.bf(v_tm, 512 * st_g, 512)
            if eng == "vector":
                OP("vector", lambda v, pa=pa, dst=dst: v.tensor_copy(dst(), pa[:, 0:512]),
                   r=[pb], w=[v_tm])
            else:
                OP("scalar", lambda s, pa=pa, dst=dst: s.copy(dst(), pa[:, 0:512]),
                   r=[pb], w=[v_tm])

    for b in xt + ln1tm + [ln1grp, wkb, wvb]:
        A.free(b)

    pt_sup2 = [A.alloc(f"pt_sup{i}", 4096) for i in range(2)]
    attnout = A.alloc("attnout", 8192)
    wob = [A.alloc("wob0", 4096)]
    OP("gpsimd", lambda g: g.dma_start(out=A.bf(wob[0]), in_=wo_h[0, :, :]),
       w=[wob[0]], dma="dwo0")

    onescol = A.alloc("onescol", 2)
    k2scr = [A.alloc(f"k2scr{i}", 512) for i in range(2)]
    kn2row = A.alloc("kn2row", 2048)
    kn2m4 = A.alloc("kn2m4", 4)
    OP("vector", lambda v: v.memset(A.f32(onescol, 0, 1), 1.0), w=[onescol])
    for j in range(NKV):
        for ch in range(4):
            k2 = k2scr[ch % 2]
            OP("gpsimd", lambda g, j=j, ch=ch, k2=k2: g.tensor_tensor(
                A.f32(k2), A.bf(k_fm, 2048 * j + 512 * ch, 512),
                A.bf(k_fm, 2048 * j + 512 * ch, 512), ALU.mult),
                r=[k_fm], w=[k2])
            OP("tensor", lambda t, k2=k2: t.matmul(
                psf_ap[4][0:1, 0:512], A.f32(onescol, 0, 1), A.f32(k2),
                start=True, stop=True), r=[onescol, k2], w=[psf_b[4]])
            OP("vector", lambda v, ch=ch: v.tensor_copy(
                A.f32(kn2row, 512 * ch, 512)[0:1, :], psf_ap[4][0:1, 0:512]),
                r=[psf_b[4]], w=[kn2row])
        OP("vector", lambda v, j=j: v.tensor_reduce(
            A.f32(kn2m4, j, 1)[0:1, :], A.f32(kn2row)[0:1, :], AX.X, ALU.max),
            r=[kn2row], w=[kn2m4])
    A.free(kn2row)
    for b in k2scr:
        A.free(b)

    q2h = [A.alloc("q2h0", 512)]
    negbh = [A.alloc(f"negbh{i}", 2) for i in range(NH)]
    qm2 = A.alloc("qm2", 4)
    onescol_b = A.alloc("onescol_b", 2)
    diag = [A.alloc(f"diag{i}", 64) for i in range(8)]
    ssc = [A.alloc(f"ssc{i}", 8) for i in range(8)]
    pstr = [A.alloc(f"p{i}", 64 * MLIST[i]) for i in range(8)]

    OP("vector", lambda v: v.memset(A.bf(onescol_b, 0, 1), 1.0), w=[onescol_b])
    for h0 in range(NH):
        j0 = h0 // 4
        q2 = q2h[0]
        OP("gpsimd", lambda g, h0=h0, q2=q2: g.tensor_tensor(
            A.bf(q2), A.bf(q_fm, 1024 * h0, 1024), A.bf(q_fm, 1024 * h0, 1024),
            ALU.mult), r=[q_fm], w=[q2])
        for c2 in range(2):
            OP("tensor", lambda t, q2=q2, c2=c2: t.matmul(
                psf_ap[5][0:1, 0:512], A.bf(onescol_b, 0, 1), A.bf(q2, 512 * c2, 512),
                start=True, stop=True), r=[onescol_b, q2], w=[psf_b[5]])
            OP("vector", lambda v, h0=h0, c2=c2: v.tensor_reduce(
                A.f32(qm2, c2, 1)[0:1, :], psf_ap[5][0:1, 0:512], AX.X, ALU.max),
                r=[psf_b[5]], w=[qm2])
        OP("vector", lambda v: v.tensor_reduce(
            A.f32(qm2, 2, 1)[0:1, :], A.f32(qm2, 0, 2)[0:1, :], AX.X, ALU.max),
            r=[qm2], w=[qm2])
        OP("vector", lambda v, j0=j0: v.tensor_tensor(
            A.f32(qm2, 3, 1)[0:1, :], A.f32(qm2, 2, 1)[0:1, :],
            A.f32(kn2m4, j0, 1)[0:1, :], ALU.add), r=[qm2, kn2m4], w=[qm2])
        OP("tensor", lambda t: t.matmul(
            psf_ap[5][:, 0:1], A.f32(onesf)[0:1, :], A.f32(qm2, 3, 1)[0:1, :],
            start=True, stop=True), r=[onesf, qm2], w=[psf_b[5]])
        OP("vector", lambda v, h0=h0: v.tensor_scalar(
            A.f32(negbh[h0], 0, 1), psf_ap[5][:, 0:1], -SCALE / 2.0, None,
            ALU.mult), r=[psf_b[5]], w=[negbh[h0]])

    def attn_stage1a(h, gi, par):
        j = h // 4
        ui = 2 * h + gi
        slots = range(4 * gi, 4 * gi + 4)
        for il, i in enumerate(slots):
            sl = 4 * par + il
            m = MLIST[i]
            L = 128 * m
            mstart = L - 256
            pbuf = pstr[sl]
            nbu, nbc = negbh[h], 0
            ci = 0
            for sc0 in range(0, L, 1024):
                pw = min(1024, L - sc0)
                pp = kc_ctr[0] % 2
                kc_ctr[0] += 1
                for sc in range(sc0, sc0 + pw, 512):
                    w_ = min(512, L - sc)
                    pb = psf_b[2 * pp + ((sc - sc0) // 512)]
                    pa = psf_ap[2 * pp + ((sc - sc0) // 512)]
                    OP("tensor", lambda t, h=h, i=i, j=j, sc=sc, w_=w_, pa=pa:
                       t.matmul(
                           pa[:, 0:w_], A.bf(q_fm, 1024 * h + 128 * i, 128),
                           A.bf(k_fm, 2048 * j + sc, w_), start=True, stop=True),
                        r=[q_fm, k_fm], w=[pb])
                if sc0 + pw == L:
                    mo_ = mstart - sc0
                    mb = [psf_b[2 * pp + (mo_ // 512)], psf_b[2 * pp + ((mo_ + 255) // 512)]]
                    OP("vector", lambda v, i=i, pp=pp, mo=mstart - sc0:
                       v.scalar_tensor_tensor(
                           psfp_ap[pp][:, mo:mo + 256], psfp_ap[pp][:, mo:mo + 256],
                           1.0, A.f32(qmask, 256 * i, 256), ALU.mult, ALU.add),
                        r=[qmask] + mb, w=mb)
                rbs = ([psf_b[2 * pp], psf_b[2 * pp + 1]] if pw > 512
                       else [psf_b[2 * pp]])
                OP("scalar", lambda s, sc0=sc0, pw=pw, pp=pp, nbu=nbu, nbc=nbc,
                   sl=sl, ci=ci, pbuf=pbuf: s.activation(
                       A.bf(pbuf, sc0, pw), psfp_ap[pp][:, 0:pw], AF.Exp,
                       bias=A.f32(nbu, nbc, 1), scale=SCALE,
                       accum_out=A.f32(ssc[sl], ci, 1)),
                    r=rbs + [nbu], w=[pbuf, ssc[sl]])
                ci += 1
            slot_ci[sl] = ci

    def attn_stage1b(h, gi, par):
        slots = range(4 * gi, 4 * gi + 4)
        for il, i in enumerate(slots):
            sl = 4 * par + il
            L = 128 * MLIST[i]
            pbuf = pstr[sl]
            ci = slot_ci[sl]
            OP("vector", lambda v, sl=sl, ci=ci: v.tensor_reduce(
                A.f32(ssum[sl], 0, 1), A.f32(ssc[sl], 0, ci), AX.X, ALU.add),
                r=[ssc[sl]], w=[ssum[sl]])
            OP("vector", lambda v, sl=sl: v.reciprocal(
                A.f32(invs[sl], 0, 1), A.f32(ssum[sl], 0, 1)),
                r=[ssum[sl]], w=[invs[sl]])
            OP("vector", lambda v, L=L, sl=sl, pbuf=pbuf: v.tensor_scalar(
                A.bf(pbuf, 0, L), A.bf(pbuf, 0, L), A.f32(invs[sl], 0, 1),
                None, ALU.mult), r=[pbuf, invs[sl]], w=[pbuf])

    def attn_stage2a(h, gi, par):
        j = h // 4
        slots = range(4 * gi, 4 * gi + 4)
        n_st = MLIST[4 * gi + 3]
        pt_sup = pt_sup2[(2 * h + gi) % 2]
        for t0 in range(0, n_st, 2):
            bk = (t0 // 2) % 2
            ts = [t0] if t0 + 1 >= n_st else [t0, t0 + 1]
            for tpos, t in enumerate(ts):
                for il, i in enumerate(slots):
                    if MLIST[i] > t:
                        sl = 4 * par + il
                        OP("tensor", lambda te, t=t, sl=sl, il=il, tpos=tpos, bk=bk:
                           te.transpose(
                               psb_ap[bk][:, 512 * tpos + 128 * il: 512 * tpos + 128 * (il + 1)],
                               A.bf(pstr[sl], 128 * t, 128), A.bf(identb)),
                           r=[pstr[sl], identb], w=[psb_b[bk]])
            ncols = 512 * len(ts)
            OP("vector", lambda v, t0=t0, nc_=ncols, bk=bk: v.tensor_copy(
                A.bf(pt_sup, 512 * t0, nc_), psb_ap[bk][:, 0:nc_]),
                r=[psb_b[bk]], w=[pt_sup])
    def attn_stage2b(h, gi, par):
        j = h // 4
        slots = range(4 * gi, 4 * gi + 4)
        n_st = MLIST[4 * gi + 3]
        pt_sup = pt_sup2[(2 * h + gi) % 2]
        pvb = psf_b[4]
        pva = psf_ap[4]
        for t in range(n_st):
            start_col = 0
            for il, i in enumerate(slots):
                if MLIST[i] > t:
                    start_col = 128 * il
                    break
            OP("tensor", lambda te, t=t, j=j, sc=start_col, pva=pva: te.matmul(
                pva[:, sc:512], A.bf(v_tm, 512 * t + 128 * j, 128),
                A.bf(pt_sup, 512 * t + sc, 512 - sc),
                start=(t == 0), stop=(t == n_st - 1)),
                r=[v_tm, pt_sup], w=[pvb])
        OP("vector", lambda v, h=h, gi=gi, pva=pva: v.tensor_copy(
            A.bf(attnout, 1024 * h + 512 * gi, 512), pva[:, 0:512]),
            r=[pvb], w=[attnout])

    slot_ci = {}
    kc_ctr = [0]
    units = [(h, gi) for h in range(NH) for gi in range(2)]
    nu = len(units)
    for ui in range(nu + 2):
        if ui < nu:
            attn_stage1a(*units[ui], ui % 2)
        if 1 <= ui <= nu:
            attn_stage2a(*units[ui - 1], (ui - 1) % 2)
        if ui < nu:
            attn_stage1b(*units[ui], ui % 2)
        if 2 <= ui <= nu + 1:
            attn_stage2b(*units[ui - 2], (ui - 2) % 2)

    for b in pstr + q2h + diag:
        A.free(b)
    A.free(onescol)
    for b in pt_sup2:
        A.free(b)
    A.free(q_fm)
    A.free(k_fm)
    A.free(v_tm)
    A.free(qmask)

    wob.append(A.alloc("wob1", 4096))
    x1 = [A.alloc(f"x1_{tt}", 2048) for tt in range(8)]
    xs = [A.alloc(f"xs{i}", 512) for i in range(4)]

    for fs in range(4):
        if fs < 3:
            nxt = wob[(fs + 1) % 2]
            OP("gpsimd", lambda g, fs=fs, nxt=nxt: g.dma_start(
                out=A.bf(nxt), in_=wo_h[fs + 1, :, :]),
                w=[nxt], dma=f"dwo{(fs + 1) % 2}")
        wb = wob[fs % 2]
        for tt in range(8):
            sb = xs[tt % 4]
            OP("sync", lambda e, fs=fs, tt=tt, sb=sb: e.dma_start(
                out=A.f32(sb), in_=x_own[128 * tt:128 * (tt + 1), 512 * fs:512 * (fs + 1)]),
                w=[sb], dma=f"dxs{tt % 4}")
            pb = psf_b[tt % 4]
            pa = psf_ap[tt % 4]
            for d in range(KC):
                OP("tensor", lambda t, d=d, tt=tt, wb=wb, pa=pa: t.matmul(
                    pa[:, 0:512], A.bf(attnout, 1024 * d + 128 * tt, 128),
                    A.bf(wb, 512 * d, 512), start=(d == 0), stop=(d == KC - 1)),
                    r=[attnout, wb], w=[pb])
            OP("vector", lambda v, fs=fs, tt=tt, sb=sb, pa=pa: v.tensor_tensor(
                A.f32(x1[tt], 512 * fs, 512), pa[:, 0:512], A.f32(sb), ALU.add),
                r=[pb, sb], w=[x1[tt]])
    A.free(attnout)
    for b in wob + xs:
        A.free(b)

    ln2tm = [A.alloc(f"ln2tm{i}", 1024) for i in range(4)]
    ln2fm = A.alloc("ln2fm", 8192)
    for tg in range(2):
        for tl in range(4):
            tt = 4 * tg + tl
            emit_ln(lambda tt=tt: A.f32(x1[tt]), x1[tt],
                    lambda tl=tl: A.bf(ln2tm[tl]), ln2tm[tl], tt % 2)
        emit_tm_to_fm(ln2tm, lambda tl: A.bf(ln2tm[tl]), ln2fm,
                      lambda k, tg=tg: A.bf(ln2fm, 1024 * k + 512 * tg, 512), 4)
    for b in ln2tm:
        A.free(b)

    dsb = A.alloc("dsb", 2048)
    dsrow = A.alloc("dsrow", 2048)
    OP("gpsimd", lambda g: g.dma_start(
        out=A.f32(dsrow)[0:1, :], in_=ds_h[None, :]), w=[dsrow], dma="ddw0")
    for q in range(4):
        OP("tensor", lambda t, q=q: t.matmul(
            psf_ap[2 + (q % 2)][:, 0:512], A.f32(onesf)[0:1, :],
            A.f32(dsrow)[0:1, 512 * q:512 * (q + 1)], start=True, stop=True),
            r=[onesf, dsrow], w=[psf_b[2 + (q % 2)]])
        OP("vector", lambda v, q=q: v.tensor_copy(
            A.f32(dsb, 512 * q, 512), psf_ap[2 + (q % 2)][:, 0:512]),
            r=[psf_b[2 + (q % 2)]], w=[dsb])
    A.free(dsrow)
    hdn = A.alloc("hdn", 7680)
    gwb = [A.alloc(f"gwb{i}", 1024) for i in range(3)]
    upb = [A.alloc(f"upb{i}", 1024) for i in range(3)]
    dwb = [A.alloc(f"dwb{i}", 256) for i in range(4)]
    silu = [A.alloc(f"silu{i}", 512) for i in range(2)]

    thirds = [(0, 15), (15, 30), (30, 44)]
    for c0, c1 in thirds:
        for c in range(c0, c1):
            cl = c - c0
            gb, ub = gwb[c % 3], upb[c % 3]
            OP("gpsimd", lambda g, c=c, gb=gb: g.dma_start(
                out=A.bf(gb), in_=gwT_h[c, :, :]), w=[gb], dma=f"dgw{c % 3}")
            OP("gpsimd", lambda g, c=c, ub=ub: g.dma_start(
                out=A.bf(ub), in_=upT_h[c, :, :]), w=[ub], dma=f"dup{c % 3}")
            for tg in range(2):
                pg = psf_b[(2 * tg) % 4]
                pga = psf_ap[(2 * tg) % 4]
                pu = psf_b[(2 * tg + 1) % 4]
                pua = psf_ap[(2 * tg + 1) % 4]
                for k in range(KC):
                    OP("tensor", lambda t, k=k, tg=tg, gb=gb, pga=pga: t.matmul(
                        pga[:, 0:512], A.bf(gb, 128 * k, 128),
                        A.bf(ln2fm, 1024 * k + 512 * tg, 512),
                        start=(k == 0), stop=(k == KC - 1)), r=[gb, ln2fm], w=[pg])
                for k in range(KC):
                    OP("tensor", lambda t, k=k, tg=tg, ub=ub, pua=pua: t.matmul(
                        pua[:, 0:512], A.bf(ub, 128 * k, 128),
                        A.bf(ln2fm, 1024 * k + 512 * tg, 512),
                        start=(k == 0), stop=(k == KC - 1)), r=[ub, ln2fm], w=[pu])
                sl = silu[tg % 2]
                OP("scalar", lambda s, c=c, sl=sl, pga=pga: s.activation(
                    A.f32(sl), pga[:, 0:512], AF.Silu, bias=A.f32(zcol, 0, 1),
                    scale=A.f32(gs_sb, c, 1)), r=[pg, gs_sb, zcol], w=[sl])
                OP("vector", lambda v, c=c, cl=cl, tg=tg, sl=sl, pua=pua:
                   v.scalar_tensor_tensor(
                       A.bf(hdn, 1024 * cl + 512 * tg, 512), pua[:, 0:512],
                       A.f32(us_sb, c, 1), A.f32(sl), ALU.mult, ALU.mult),
                   r=[pu, us_sb, sl], w=[hdn])
        nch = c1 - c0
        for hs in range(4):
            for c in range(c0, c1):
                cl = c - c0
                db = dwb[c % 4]
                OP("gpsimd", lambda g, c=c, hs=hs, db=db: g.dma_start(
                    out=A.bf(db), in_=dwT_h[128 * c:128 * (c + 1), 512 * hs:512 * (hs + 1)]),
                    w=[db], dma=f"ddw{c % 4}")
                OP("vector", lambda v, hs=hs, db=db: v.tensor_tensor(
                    A.bf(db), A.bf(db), A.f32(dsb, 512 * hs, 512), ALU.mult),
                    r=[db, dsb], w=[db])
                for tt in range(8):
                    pd = psf_b[tt] if tt < 6 else psb_b[tt - 6]
                    pda = psf_ap[tt][:, 0:512] if tt < 6 else \
                        psb_ap[tt - 6][:, 0:1024].bitcast(F32)[:, 0:512]
                    OP("tensor", lambda t, cl=cl, tt=tt, db=db, pda=pda, nch=nch: t.matmul(
                        pda, A.bf(hdn, 1024 * cl + 128 * tt, 128), A.bf(db),
                        start=(cl == 0), stop=(cl == nch - 1)),
                        r=[hdn, db], w=[pd])
            for tt in range(8):
                pd = psf_b[tt] if tt < 6 else psb_b[tt - 6]
                pda = psf_ap[tt][:, 0:512] if tt < 6 else \
                    psb_ap[tt - 6][:, 0:1024].bitcast(F32)[:, 0:512]
                OP("vector", lambda v, hs=hs, tt=tt, pda=pda: v.tensor_tensor(
                    A.f32(x1[tt], 512 * hs, 512), pda, A.f32(x1[tt], 512 * hs, 512),
                    ALU.add), r=[pd, x1[tt]], w=[x1[tt]])

    ybufs = []
    for tt in range(8):
        yb = Buf(f"y{tt}")
        ybufs.append(yb)
        OP("sync", lambda e, tt=tt: e.dma_start(
            out=y_h[128 * tt:128 * (tt + 1), :], in_=A.f32(x1[tt])),
            r=[x1[tt]], w=[yb], dma="dout")

    for nm, shp in debug:
        pass

    OP("sync", lambda e: e.nop(), r=ybufs)

    for o in bld.ops:
        if o.dma_sem in ("dinit", "dwkv"):
            o.dma_val = 16 * bld.dma_counts[o.dma_sem]

    bld.finalize()

    with nc.Block() as block:
        for eng_name in ENGINES:
            section = getattr(block, eng_name)

            @section
            def _(eng, _e=eng_name):
                bld.emit_stream(_e, eng, sems)

    es.close()
    return nc


def _make_qmask(par):
    qm = np.zeros((8, P, 256), dtype=np.float32)
    tri = np.where(np.arange(P)[None, :] > np.arange(P)[:, None], NEG, 0.0).astype(np.float32)
    for i in range(8):
        m = MLIST[i]
        t_true = QT_PAR[par][i] + 1
        if t_true == m:
            qm[i, :, 128:256] = tri
        else:
            qm[i, :, 0:128] = tri
            qm[i, :, 128:256] = NEG
    return qm


_NC_CACHE = {}


def kernel(L0_attn_x, L0_attn_Wq, L0_attn_Wk, L0_attn_Wv, L0_attn_Wo,
           L0_ffn_gate_w, L0_ffn_up_w, L0_ffn_down_w,
           L0_ffn_gate_s, L0_ffn_up_s, L0_ffn_down_s):
    x = np.ascontiguousarray(np.asarray(L0_attn_x, dtype=np.float32))
    wq0 = np.asarray(L0_attn_Wq, dtype=np.float32)
    wk0 = np.asarray(L0_attn_Wk, dtype=np.float32)
    wv0 = np.asarray(L0_attn_Wv, dtype=np.float32)
    wo0 = np.asarray(L0_attn_Wo, dtype=np.float32)
    wq = np.ascontiguousarray(
        wq0.reshape(KC, P, KC, P).transpose(2, 1, 0, 3).reshape(KC, P, D))
    wk = np.ascontiguousarray(wk0.reshape(KC, P, 512).transpose(1, 0, 2).reshape(P, KC * 512))
    wv = np.ascontiguousarray(wv0.reshape(KC, P, 512).transpose(1, 0, 2).reshape(P, KC * 512))
    wo = np.ascontiguousarray(
        wo0.reshape(KC, P, 4, 512).transpose(2, 1, 0, 3).reshape(4, P, KC * 512))
    gwT0 = np.asarray(L0_ffn_gate_w, dtype=np.float32).T
    upT0 = np.asarray(L0_ffn_up_w, dtype=np.float32).T
    gwT = np.ascontiguousarray(
        gwT0.reshape(KC, P, NI, P).transpose(2, 1, 0, 3).reshape(NI, P, KC * P))
    upT = np.ascontiguousarray(
        upT0.reshape(KC, P, NI, P).transpose(2, 1, 0, 3).reshape(NI, P, KC * P))
    dwT = np.ascontiguousarray(np.asarray(L0_ffn_down_w, dtype=np.float32).T)
    gs = np.ascontiguousarray(np.asarray(L0_ffn_gate_s, dtype=np.float32))
    us = np.ascontiguousarray(np.asarray(L0_ffn_up_s, dtype=np.float32))
    ds = np.ascontiguousarray(np.asarray(L0_ffn_down_s, dtype=np.float32))

    qmasks = [_make_qmask(0), _make_qmask(1)]
    row_idx = [np.concatenate([np.arange(128 * g, 128 * (g + 1)) for g in QT_PAR[p]])
               for p in range(2)]

    in_maps = []
    for c in range(8):
        b, par = c // 2, c % 2
        in_maps.append({
            "x_full": x[b],
            "x_own": np.ascontiguousarray(x[b][row_idx[par]]),
            "wq": wq, "wk": wk, "wv": wv, "wo": wo,
            "gwT": gwT, "upT": upT, "dwT": dwT,
            "gs": gs, "us": us, "ds": ds,
            "qm": qmasks[par],
        })

    nc = _NC_CACHE.get("nc")
    if nc is None:
        nc = build_program()
        _NC_CACHE["nc"] = nc

    res = run_bass_kernel_spmd(nc, in_maps, core_ids=list(range(8)))

    out = np.empty((4, S, D), dtype=np.float32)
    for c in range(8):
        b, par = c // 2, c % 2
        out[b][row_idx[par]] = res.results[c]["y"]
    return out


if __name__ == "__main__":
    rng = np.random.default_rng(0)
    ins = {
        "L0_attn_x": rng.standard_normal((4, S, D), dtype=np.float32),
        "L0_attn_Wq": rng.standard_normal((D, D), dtype=np.float32) * 0.02,
        "L0_attn_Wk": rng.standard_normal((D, 512), dtype=np.float32) * 0.02,
        "L0_attn_Wv": rng.standard_normal((D, 512), dtype=np.float32) * 0.02,
        "L0_attn_Wo": rng.standard_normal((D, D), dtype=np.float32) * 0.02,
        "L0_ffn_gate_w": rng.integers(-1, 2, (INTER, D)).astype(np.float32),
        "L0_ffn_up_w": rng.integers(-1, 2, (INTER, D)).astype(np.float32),
        "L0_ffn_down_w": rng.integers(-1, 2, (D, INTER)).astype(np.float32),
        "L0_ffn_gate_s": rng.uniform(0.01, 0.1, INTER).astype(np.float32),
        "L0_ffn_up_s": rng.uniform(0.01, 0.1, INTER).astype(np.float32),
        "L0_ffn_down_s": rng.uniform(0.01, 0.1, D).astype(np.float32),
    }
    y = kernel(**ins)
    print("ran, y:", y.shape, y.dtype, np.abs(y).mean())


# version 14
# speedup vs baseline: 1.0318x; 1.0318x over previous
import numpy as np
from contextlib import ExitStack

import concourse.bass as bass
import concourse.mybir as mybir
from concourse.bass_utils import run_bass_kernel_spmd

F32 = mybir.dt.float32
BF16 = mybir.dt.bfloat16
AF = mybir.ActivationFunctionType
ALU = mybir.AluOpType
AX = mybir.AxisListType

P = 128
D = 2048
S = 2048
TOWN = 1024
NKV = 4
NH = 16
INTER = 5632
NI = 44
KC = 16
SCALE = float(D // NH) ** -0.5
MLIST = [2, 4, 6, 8, 10, 12, 14, 16]
QT_PAR = [[0, 3, 4, 7, 8, 11, 12, 15],
          [1, 2, 5, 6, 9, 10, 13, 14]]
NEG = -1.0e30
ARENA_W = 48000

ENGINES = ("sync", "gpsimd", "vector", "scalar", "tensor")


class Buf:
    __slots__ = ("name", "off", "words", "init_deps")

    def __init__(self, name, off=-1, words=0):
        self.name = name
        self.off = off
        self.words = words
        self.init_deps = None


class Op:
    __slots__ = ("idx", "eng", "emit", "deps", "kind", "dma_sem", "dma_val",
                 "needs_inc", "inc_val", "clock")

    def __init__(self, idx, eng, emit, kind, dma_sem=None, dma_val=0):
        self.idx = idx
        self.eng = eng
        self.emit = emit
        self.deps = {}
        self.kind = kind
        self.dma_sem = dma_sem
        self.dma_val = dma_val
        self.needs_inc = False
        self.inc_val = 0
        self.clock = None


class Builder:
    def __init__(self):
        self.ops = []
        self.state = {}
        self.dma_counts = {}

    def _st(self, b):
        s = self.state.get(id(b))
        if s is None:
            s = [None, [], b]
            self.state[id(b)] = s
        return s

    def op(self, eng, emit, r=(), w=(), dma=None):
        idx = len(self.ops)
        if dma is not None:
            cnt = self.dma_counts.get(dma, 0) + 1
            self.dma_counts[dma] = cnt
            o = Op(idx, eng, emit, "d", dma, 16 * cnt)
        else:
            o = Op(idx, eng, emit, "c")
        deps = o.deps

        def add(d, typ):
            if d is None:
                return
            old = deps.get(d)
            if old is None or (old != "raw" and typ == "raw"):
                deps[d] = typ

        for b in r:
            st = self._st(b)
            if b.init_deps:
                for d in b.init_deps:
                    add(d, "raw")
                b.init_deps = None
            add(st[0], "raw")
            st[1].append(idx)
        for b in w:
            st = self._st(b)
            if b.init_deps:
                for d in b.init_deps:
                    add(d, "waw")
                b.init_deps = None
            add(st[0], "waw")
            for d in st[1]:
                add(d, "war")
            st[0] = idx
            st[1] = []
        self.ops.append(o)
        return idx

    def events_of(self, b):
        s = self.state.get(id(b))
        if s is None:
            return []
        ev = list(s[1])
        if s[0] is not None:
            ev.append(s[0])
        return ev

    def finalize(self):
        ops = self.ops
        for x in ops:
            kept = {}
            for d, typ in x.deps.items():
                o = ops[d]
                if o.kind == "c" and x.kind == "c" and o.eng == x.eng:
                    if o.eng == "tensor":
                        continue
                    if typ != "raw":
                        continue
                kept[d] = typ
                if o.kind == "c":
                    o.needs_inc = True
            x.deps = kept
        cnt = {e: 0 for e in ENGINES}
        for o in ops:
            if o.kind == "c":
                if o.needs_inc:
                    cnt[o.eng] += 1
                o.inc_val = cnt[o.eng]

    def emit_stream(self, eng_name, eng, sems):
        know = {}
        own = "E_" + eng_name
        for o in self.ops:
            if o.eng != eng_name:
                continue
            need = {}
            for d in o.deps:
                p = self.ops[d]
                if p.kind == "d":
                    sem, val = p.dma_sem, p.dma_val
                else:
                    sem, val = "E_" + p.eng, p.inc_val
                    if sem == own and p.eng == eng_name and False:
                        pass
                if know.get(sem, 0) < val:
                    if need.get(sem, 0) < val:
                        need[sem] = val
                pc = p.clock
                if pc:
                    for s2, v2 in pc.items():
                        if know.get(s2, 0) < v2:
                            know[s2] = v2
                if know.get(sem, 0) < val:
                    know[sem] = val
            for sem, val in need.items():
                eng.wait_ge(sems[sem], val)
            inst = o.emit(eng)
            if o.kind == "d":
                inst.then_inc(sems[o.dma_sem], 16)
                o.clock = dict(know)
            elif o.needs_inc:
                inst.then_inc(sems[own], 1)
                o.clock = dict(know)


class Arena:
    def __init__(self, ap, words, bld):
        self.ap = ap
        self.bld = bld
        self.blocks = [(0, words, ())]

    def alloc(self, name, words):
        words = (words + 1) & ~1
        for i, (st, sz, gh) in enumerate(self.blocks):
            if sz >= words:
                b = Buf(name, st, words)
                b.init_deps = list(gh) if gh else None
                if sz == words:
                    self.blocks.pop(i)
                else:
                    self.blocks[i] = (st + words, sz - words, gh)
                return b
        raise RuntimeError(f"arena OOM for {name} ({words} words); free={self.blocks}")

    def f32(self, b, c0=0, cn=None):
        cn = b.words if cn is None else cn
        return self.ap[:, b.off + c0 : b.off + c0 + cn]

    def bf(self, b, c0=0, cn=None):
        full = self.ap[:, b.off : b.off + b.words].bitcast(BF16)
        cn = 2 * b.words if cn is None else cn
        return full[:, c0 : c0 + cn]

    def free(self, b):
        ev = tuple(self.bld.events_of(b))
        blocks = self.blocks
        st, sz = b.off, b.words
        pos = 0
        while pos < len(blocks) and blocks[pos][0] < st:
            pos += 1
        blocks.insert(pos, (st, sz, ev))
        merged = []
        for blk in blocks:
            if merged and merged[-1][0] + merged[-1][1] == blk[0]:
                a = merged[-1]
                merged[-1] = (a[0], a[1] + blk[1], tuple(set(a[2]) | set(blk[2])))
            else:
                merged.append(blk)
        self.blocks = merged


def build_program(debug=()):
    nc = bass.Bass("TRN2", target_bir_lowering=False)

    x_full = nc.dram_tensor("x_full", [S, D], F32, kind="ExternalInput").ap()
    x_own = nc.dram_tensor("x_own", [TOWN, D], F32, kind="ExternalInput").ap()
    wq_h = nc.dram_tensor("wq", [KC, P, D], F32, kind="ExternalInput").ap()
    wk_h = nc.dram_tensor("wk", [P, KC * 512], F32, kind="ExternalInput").ap()
    wv_h = nc.dram_tensor("wv", [P, KC * 512], F32, kind="ExternalInput").ap()
    wo_h = nc.dram_tensor("wo", [4, P, KC * 512], F32, kind="ExternalInput").ap()
    gwT_h = nc.dram_tensor("gwT", [NI, P, KC * 128], F32, kind="ExternalInput").ap()
    upT_h = nc.dram_tensor("upT", [NI, P, KC * 128], F32, kind="ExternalInput").ap()
    dwT_h = nc.dram_tensor("dwT", [INTER, D], F32, kind="ExternalInput").ap()
    gs_h = nc.dram_tensor("gs", [INTER], F32, kind="ExternalInput").ap()
    us_h = nc.dram_tensor("us", [INTER], F32, kind="ExternalInput").ap()
    ds_h = nc.dram_tensor("ds", [D], F32, kind="ExternalInput").ap()
    qm_h = nc.dram_tensor("qm", [8, P, 256], F32, kind="ExternalInput").ap()
    y_h = nc.dram_tensor("y", [TOWN, D], F32, kind="ExternalOutput").ap()

    dbg_h = {}
    for nm, shp in debug:
        dbg_h[nm] = nc.dram_tensor("dbg_" + nm, list(shp), F32, kind="ExternalOutput").ap()

    bld = Builder()
    es = ExitStack()

    arena_t = es.enter_context(nc.sbuf_tensor("arena", [P, ARENA_W], F32))
    A = Arena(arena_t.ap(), ARENA_W, bld)

    psfp = [es.enter_context(nc.psum_tensor(f"psfp{i}", [P, 1024], F32)) for i in range(2)]
    psf45 = [es.enter_context(nc.psum_tensor(f"psf{i}", [P, 512], F32)) for i in range(4, 6)]
    psb = [es.enter_context(nc.psum_tensor(f"psb{i}", [P, 1024], BF16)) for i in range(2)]
    psf_b = [Buf(f"psf{i}") for i in range(6)]
    psb_b = [Buf(f"psb{i}") for i in range(2)]
    psfp_ap = [t.ap() for t in psfp]
    psf_ap = [psfp_ap[0][:, 0:512], psfp_ap[0][:, 512:1024],
              psfp_ap[1][:, 0:512], psfp_ap[1][:, 512:1024],
              psf45[0].ap(), psf45[1].ap()]
    psb_ap = [t.ap() for t in psb]

    sem_names = (["E_gpsimd", "E_vector", "E_scalar", "E_tensor", "E_sync"]
                 + ["dinit", "dwkv", "dxf0", "dxf1", "dwq0", "dwq1",
                    "dwo0", "dwo1", "dgw0", "dgw1", "dgw2", "dup0", "dup1", "dup2",
                    "ddw0", "ddw1", "ddw2", "ddw3", "dxs0", "dxs1", "dxs2", "dxs3", "dout"])
    sems = {nm: es.enter_context(nc.semaphore(nm)) for nm in sem_names}

    identb = A.alloc("identb", 64)
    identf = A.alloc("identf", 128)
    onesf = A.alloc("onesf", 128)
    zcol = A.alloc("zcol", 2)
    epscol = A.alloc("epscol", 2)
    gs_c = A.alloc("gs_c", 128)
    gs_sb = A.alloc("gs_sb", 44)
    us_c = A.alloc("us_c", 128)
    us_sb = A.alloc("us_sb", 44)
    qmask = A.alloc("qmask", 2048)
    stv = [A.alloc(f"st{i}", 24) for i in range(2)]
    mvv = [A.alloc(f"mv{i}", 2) for i in range(2)]
    lnvv = [A.alloc(f"lnv{i}", 2) for i in range(2)]
    rstdv = [A.alloc(f"rstd{i}", 2) for i in range(2)]
    negm = [A.alloc(f"negm{i}", 2) for i in range(8)]
    ssum = [A.alloc(f"ssum{i}", 2) for i in range(8)]
    invs = [A.alloc(f"invs{i}", 2) for i in range(8)]

    OP = bld.op

    OP("gpsimd", lambda g: g.memset(A.bf(identb), 0.0), w=[identb])
    OP("gpsimd", lambda g: g.affine_select(
        out=A.bf(identb), in_=A.bf(identb), compare_op=ALU.not_equal,
        fill=1.0, base=0, pattern=[[-1, P]], channel_multiplier=1),
        r=[identb], w=[identb])
    OP("gpsimd", lambda g: g.memset(A.f32(identf), 0.0), w=[identf])
    OP("gpsimd", lambda g: g.affine_select(
        out=A.f32(identf), in_=A.f32(identf), compare_op=ALU.not_equal,
        fill=1.0, base=0, pattern=[[-1, P]], channel_multiplier=1),
        r=[identf], w=[identf])
    OP("vector", lambda v: v.memset(A.f32(onesf)[0:1, :], 1.0), w=[onesf])
    OP("vector", lambda v: v.memset(A.f32(zcol, 0, 1), 0.0), w=[zcol])
    OP("vector", lambda v: v.memset(A.f32(epscol, 0, 1), 1.0e-5), w=[epscol])

    OP("gpsimd", lambda g: g.dma_start(
        out=A.f32(gs_c)[0:44, :],
        in_=gs_h.rearrange("(c p) -> c p", p=P)), w=[gs_c], dma="dinit")
    OP("gpsimd", lambda g: g.dma_start(
        out=A.f32(us_c)[0:44, :],
        in_=us_h.rearrange("(c p) -> c p", p=P)), w=[us_c], dma="dinit")
    OP("gpsimd", lambda g: g.dma_start(
        out=A.f32(qmask).rearrange("p (s f) -> p s f", f=256),
        in_=qm_h.rearrange("s p f -> p s f")), w=[qmask], dma="dinit")

    OP("tensor", lambda t: t.transpose(psf_ap[0][:, 0:44], A.f32(gs_c)[0:44, :],
                                       A.f32(identf)[0:44, 0:44]),
       r=[gs_c, identf], w=[psf_b[0]])
    OP("vector", lambda v: v.tensor_copy(A.f32(gs_sb), psf_ap[0][:, 0:44]),
       r=[psf_b[0]], w=[gs_sb])
    OP("tensor", lambda t: t.transpose(psf_ap[1][:, 0:44], A.f32(us_c)[0:44, :],
                                       A.f32(identf)[0:44, 0:44]),
       r=[us_c, identf], w=[psf_b[1]])
    OP("vector", lambda v: v.tensor_copy(A.f32(us_sb), psf_ap[1][:, 0:44]),
       r=[psf_b[1]], w=[us_sb])

    A.free(gs_c)
    A.free(us_c)

    def emit_ln(x_ap_fn, xbuf, out_tm_fn, outbuf, par):
        st, mv, lnv, rstd = stv[par], mvv[par], lnvv[par], rstdv[par]
        for i in range(4):
            OP("vector", lambda v, i=i: v.bn_stats(
                A.f32(st, 6 * i, 6), x_ap_fn()[:, 512 * i:512 * (i + 1)]),
                r=[xbuf], w=[st])
        OP("vector", lambda v: v.bn_aggr(A.f32(mv, 0, 2), A.f32(st, 0, 24)),
           r=[st], w=[mv])
        OP("scalar", lambda s: s.activation(
            A.f32(lnv, 0, 1), A.f32(mv, 1, 1), AF.Ln,
            bias=A.f32(epscol, 0, 1), scale=1.0), r=[mv, epscol], w=[lnv])
        OP("scalar", lambda s: s.activation(
            A.f32(rstd, 0, 1), A.f32(lnv, 0, 1), AF.Exp,
            bias=A.f32(zcol, 0, 1), scale=-0.5), r=[lnv, zcol], w=[rstd])
        OP("vector", lambda v: v.tensor_scalar(
            out_tm_fn(), x_ap_fn(), A.f32(mv, 0, 1), A.f32(rstd, 0, 1),
            ALU.subtract, ALU.mult), r=[xbuf, mv, rstd], w=[outbuf])

    def emit_tm_to_fm(tm_bufs, tm_fn, dst_buf, dst_fn, ntile, bank_off=0):
        for k in range(KC):
            bk = (k + bank_off) % 2
            for tt in range(ntile):
                OP("tensor", lambda t, k=k, tt=tt, bk=bk: t.transpose(
                    psb_ap[bk][:, 128 * tt:128 * (tt + 1)],
                    tm_fn(tt)[:, 128 * k:128 * (k + 1)], A.bf(identb)),
                    r=[tm_bufs[tt], identb], w=[psb_b[bk]])
            eng = "vector" if (k % 2 == 0) else "scalar"
            if eng == "vector":
                OP("vector", lambda v, k=k, bk=bk: v.tensor_copy(
                    dst_fn(k), psb_ap[bk][:, 0:128 * ntile]),
                    r=[psb_b[bk]], w=[dst_buf])
            else:
                OP("scalar", lambda s, k=k, bk=bk: s.copy(
                    dst_fn(k), psb_ap[bk][:, 0:128 * ntile]),
                    r=[psb_b[bk]], w=[dst_buf])

    xt = [A.alloc(f"xt{i}", 2048) for i in range(2)]
    ln1tm = [A.alloc(f"ln1tm{i}", 1024) for i in range(4)]
    ln1own = A.alloc("ln1own", 8192)
    q_fm = A.alloc("q_fm", 8192)
    k_fm = A.alloc("k_fm", 4096)
    v_tm = A.alloc("v_tm", 4096)

    wqb = [A.alloc(f"wqb{i}", 1024) for i in range(2)]
    for tg in range(2):
        for tl in range(4):
            tt = 4 * tg + tl
            xb = xt[tt % 2]
            OP("sync", lambda e, tt=tt, xb=xb: e.dma_start(
                out=A.f32(xb), in_=x_own[128 * tt:128 * (tt + 1), :]),
                w=[xb], dma=f"dxf{tt % 2}")
            emit_ln(lambda xb=xb: A.f32(xb), xb,
                    lambda tl=tl: A.bf(ln1tm[tl]), ln1tm[tl], tt % 2)
        emit_tm_to_fm(ln1tm, lambda tl: A.bf(ln1tm[tl]), ln1own,
                      lambda k, tg=tg: A.bf(ln1own, 1024 * k + 512 * tg, 512), 4)

    for f in range(KC):
        wb = wqb[f % 2]
        OP("gpsimd", lambda g, f=f, wb=wb: g.dma_start(
            out=A.bf(wb), in_=wq_h[f, :, :]), w=[wb], dma=f"dwq{f % 2}")
        for tg in range(2):
            pb = psf_b[(2 * f + tg) % 4]
            pa = psf_ap[(2 * f + tg) % 4]
            for k in range(KC):
                OP("tensor", lambda t, k=k, f=f, tg=tg, wb=wb, pa=pa: t.matmul(
                    pa[:, 0:512], A.bf(wb, 128 * k, 128),
                    A.bf(ln1own, 1024 * k + 512 * tg, 512),
                    start=(k == 0), stop=(k == KC - 1)),
                    r=[wb, ln1own], w=[pb])
            eng = "vector" if tg == 0 else "scalar"
            dst = lambda f=f, tg=tg: A.bf(q_fm, 1024 * f + 512 * tg, 512)
            if eng == "vector":
                OP("vector", lambda v, pa=pa, dst=dst, pb=pb: v.tensor_copy(dst(), pa[:, 0:512]),
                   r=[pb], w=[q_fm])
            else:
                OP("scalar", lambda sx, pa=pa, dst=dst, pb=pb: sx.copy(dst(), pa[:, 0:512]),
                   r=[pb], w=[q_fm])

    A.free(ln1own)
    for b in wqb:
        A.free(b)

    wkb = A.alloc("wkb", 4096)
    wvb = A.alloc("wvb", 4096)
    OP("gpsimd", lambda g: g.dma_start(out=A.bf(wkb), in_=wk_h[:, :]),
       w=[wkb], dma="dwkv")
    OP("gpsimd", lambda g: g.dma_start(out=A.bf(wvb), in_=wv_h[:, :]),
       w=[wvb], dma="dwkv")

    ln1grp = A.alloc("ln1grp", 4096)
    for tg in range(4):
        for tl in range(4):
            tt = 4 * tg + tl
            xb = xt[tt % 2]
            OP("sync", lambda e, tt=tt, xb=xb: e.dma_start(
                out=A.f32(xb), in_=x_full[128 * tt:128 * (tt + 1), :]),
                w=[xb], dma=f"dxf{tt % 2}")
            emit_ln(lambda xb=xb: A.f32(xb), xb,
                    lambda tl=tl: A.bf(ln1tm[tl]), ln1tm[tl], tt % 2)
        emit_tm_to_fm(ln1tm, lambda tl: A.bf(ln1tm[tl]), ln1grp,
                      lambda k: A.bf(ln1grp, 512 * k, 512), 4)
        for j in range(NKV):
            pb = psf_b[j % 2]
            pa = psf_ap[j % 2]
            for k in range(KC):
                OP("tensor", lambda t, j=j, k=k, pa=pa: t.matmul(
                    pa[:, 0:512], A.bf(wkb, 512 * k + 128 * j, 128),
                    A.bf(ln1grp, 512 * k, 512),
                    start=(k == 0), stop=(k == KC - 1)), r=[wkb, ln1grp], w=[pb])
            eng = "vector" if j % 2 == 0 else "scalar"
            dst = lambda j=j, tg=tg: A.bf(k_fm, 2048 * j + 512 * tg, 512)
            if eng == "vector":
                OP("vector", lambda v, pa=pa, dst=dst: v.tensor_copy(dst(), pa[:, 0:512]),
                   r=[pb], w=[k_fm])
            else:
                OP("scalar", lambda s, pa=pa, dst=dst: s.copy(dst(), pa[:, 0:512]),
                   r=[pb], w=[k_fm])
        for sl in range(4):
            st_g = 4 * tg + sl
            pb = psf_b[2 + sl % 2]
            pa = psf_ap[2 + sl % 2]
            for k in range(KC):
                OP("tensor", lambda t, sl=sl, k=k, pa=pa: t.matmul(
                    pa[:, 0:512], A.bf(ln1grp, 512 * k + 128 * sl, 128),
                    A.bf(wvb, 512 * k, 512),
                    start=(k == 0), stop=(k == KC - 1)), r=[wvb, ln1grp], w=[pb])
            eng = "vector" if sl % 2 == 0 else "scalar"
            dst = lambda st_g=st_g: A.bf(v_tm, 512 * st_g, 512)
            if eng == "vector":
                OP("vector", lambda v, pa=pa, dst=dst: v.tensor_copy(dst(), pa[:, 0:512]),
                   r=[pb], w=[v_tm])
            else:
                OP("scalar", lambda s, pa=pa, dst=dst: s.copy(dst(), pa[:, 0:512]),
                   r=[pb], w=[v_tm])

    for b in xt + ln1tm + [ln1grp, wkb, wvb]:
        A.free(b)

    pt_sup2 = [A.alloc(f"pt_sup{i}", 4096) for i in range(2)]
    attnout = A.alloc("attnout", 8192)
    wob = [A.alloc("wob0", 4096)]
    OP("gpsimd", lambda g: g.dma_start(out=A.bf(wob[0]), in_=wo_h[0, :, :]),
       w=[wob[0]], dma="dwo0")

    onescol = A.alloc("onescol", 2)
    k2scr = [A.alloc(f"k2scr{i}", 512) for i in range(2)]
    kn2row = A.alloc("kn2row", 2048)
    kn2m4 = A.alloc("kn2m4", 4)
    OP("vector", lambda v: v.memset(A.f32(onescol, 0, 1), 1.0), w=[onescol])
    for j in range(NKV):
        for ch in range(4):
            k2 = k2scr[ch % 2]
            OP("gpsimd", lambda g, j=j, ch=ch, k2=k2: g.tensor_tensor(
                A.f32(k2), A.bf(k_fm, 2048 * j + 512 * ch, 512),
                A.bf(k_fm, 2048 * j + 512 * ch, 512), ALU.mult),
                r=[k_fm], w=[k2])
            OP("tensor", lambda t, k2=k2: t.matmul(
                psf_ap[4][0:1, 0:512], A.f32(onescol, 0, 1), A.f32(k2),
                start=True, stop=True), r=[onescol, k2], w=[psf_b[4]])
            OP("vector", lambda v, ch=ch: v.tensor_copy(
                A.f32(kn2row, 512 * ch, 512)[0:1, :], psf_ap[4][0:1, 0:512]),
                r=[psf_b[4]], w=[kn2row])
        OP("vector", lambda v, j=j: v.tensor_reduce(
            A.f32(kn2m4, j, 1)[0:1, :], A.f32(kn2row)[0:1, :], AX.X, ALU.max),
            r=[kn2row], w=[kn2m4])
    A.free(kn2row)
    for b in k2scr:
        A.free(b)

    q2h = [A.alloc("q2h0", 512)]
    negbh = [A.alloc(f"negbh{i}", 2) for i in range(NH)]
    qm2 = A.alloc("qm2", 4)
    onescol_b = A.alloc("onescol_b", 2)
    diag = [A.alloc(f"diag{i}", 64) for i in range(8)]
    ssc = [A.alloc(f"ssc{i}", 8) for i in range(8)]
    pstr = [A.alloc(f"p{i}", 64 * MLIST[i]) for i in range(8)]

    OP("vector", lambda v: v.memset(A.bf(onescol_b, 0, 1), 1.0), w=[onescol_b])
    for h0 in range(NH):
        j0 = h0 // 4
        q2 = q2h[0]
        OP("gpsimd", lambda g, h0=h0, q2=q2: g.tensor_tensor(
            A.bf(q2), A.bf(q_fm, 1024 * h0, 1024), A.bf(q_fm, 1024 * h0, 1024),
            ALU.mult), r=[q_fm], w=[q2])
        for c2 in range(2):
            OP("tensor", lambda t, q2=q2, c2=c2: t.matmul(
                psf_ap[5][0:1, 0:512], A.bf(onescol_b, 0, 1), A.bf(q2, 512 * c2, 512),
                start=True, stop=True), r=[onescol_b, q2], w=[psf_b[5]])
            OP("vector", lambda v, h0=h0, c2=c2: v.tensor_reduce(
                A.f32(qm2, c2, 1)[0:1, :], psf_ap[5][0:1, 0:512], AX.X, ALU.max),
                r=[psf_b[5]], w=[qm2])
        OP("vector", lambda v: v.tensor_reduce(
            A.f32(qm2, 2, 1)[0:1, :], A.f32(qm2, 0, 2)[0:1, :], AX.X, ALU.max),
            r=[qm2], w=[qm2])
        OP("vector", lambda v, j0=j0: v.tensor_tensor(
            A.f32(qm2, 3, 1)[0:1, :], A.f32(qm2, 2, 1)[0:1, :],
            A.f32(kn2m4, j0, 1)[0:1, :], ALU.add), r=[qm2, kn2m4], w=[qm2])
        OP("tensor", lambda t: t.matmul(
            psf_ap[5][:, 0:1], A.f32(onesf)[0:1, :], A.f32(qm2, 3, 1)[0:1, :],
            start=True, stop=True), r=[onesf, qm2], w=[psf_b[5]])
        OP("vector", lambda v, h0=h0: v.tensor_scalar(
            A.f32(negbh[h0], 0, 1), psf_ap[5][:, 0:1], -SCALE / 2.0, None,
            ALU.mult), r=[psf_b[5]], w=[negbh[h0]])

    def attn_stage1a(h, gi, par):
        j = h // 4
        ui = 2 * h + gi
        slots = range(4 * gi, 4 * gi + 4)
        for il, i in enumerate(slots):
            sl = 4 * par + il
            m = MLIST[i]
            L = 128 * m
            mstart = L - 256
            pbuf = pstr[sl]
            nbu, nbc = negbh[h], 0
            ci = 0
            for sc0 in range(0, L, 1024):
                pw = min(1024, L - sc0)
                pp = kc_ctr[0] % 2
                kc_ctr[0] += 1
                for sc in range(sc0, sc0 + pw, 512):
                    w_ = min(512, L - sc)
                    pb = psf_b[2 * pp + ((sc - sc0) // 512)]
                    pa = psf_ap[2 * pp + ((sc - sc0) // 512)]
                    OP("tensor", lambda t, h=h, i=i, j=j, sc=sc, w_=w_, pa=pa:
                       t.matmul(
                           pa[:, 0:w_], A.bf(q_fm, 1024 * h + 128 * i, 128),
                           A.bf(k_fm, 2048 * j + sc, w_), start=True, stop=True),
                        r=[q_fm, k_fm], w=[pb])
                if sc0 + pw == L:
                    mo_ = mstart - sc0
                    mb = [psf_b[2 * pp + (mo_ // 512)], psf_b[2 * pp + ((mo_ + 255) // 512)]]
                    OP("vector", lambda v, i=i, pp=pp, mo=mstart - sc0:
                       v.scalar_tensor_tensor(
                           psfp_ap[pp][:, mo:mo + 256], psfp_ap[pp][:, mo:mo + 256],
                           1.0, A.f32(qmask, 256 * i, 256), ALU.mult, ALU.add),
                        r=[qmask] + mb, w=mb)
                rbs = ([psf_b[2 * pp], psf_b[2 * pp + 1]] if pw > 512
                       else [psf_b[2 * pp]])
                OP("scalar", lambda s, sc0=sc0, pw=pw, pp=pp, nbu=nbu, nbc=nbc,
                   sl=sl, ci=ci, pbuf=pbuf: s.activation(
                       A.bf(pbuf, sc0, pw), psfp_ap[pp][:, 0:pw], AF.Exp,
                       bias=A.f32(nbu, nbc, 1), scale=SCALE,
                       accum_out=A.f32(ssc[sl], ci, 1)),
                    r=rbs + [nbu], w=[pbuf, ssc[sl]])
                ci += 1
            slot_ci[sl] = ci

    def attn_stage1b(h, gi, par):
        slots = range(4 * gi, 4 * gi + 4)
        for il, i in enumerate(slots):
            sl = 4 * par + il
            L = 128 * MLIST[i]
            pbuf = pstr[sl]
            ci = slot_ci[sl]
            OP("vector", lambda v, sl=sl, ci=ci: v.tensor_reduce(
                A.f32(ssum[sl], 0, 1), A.f32(ssc[sl], 0, ci), AX.X, ALU.add),
                r=[ssc[sl]], w=[ssum[sl]])
            OP("vector", lambda v, sl=sl: v.reciprocal(
                A.f32(invs[sl], 0, 1), A.f32(ssum[sl], 0, 1)),
                r=[ssum[sl]], w=[invs[sl]])
            OP("vector", lambda v, L=L, sl=sl, pbuf=pbuf: v.tensor_scalar(
                A.bf(pbuf, 0, L), A.bf(pbuf, 0, L), A.f32(invs[sl], 0, 1),
                None, ALU.mult), r=[pbuf, invs[sl]], w=[pbuf])

    def attn_stage2a(h, gi, par):
        j = h // 4
        slots = range(4 * gi, 4 * gi + 4)
        n_st = MLIST[4 * gi + 3]
        pt_sup = pt_sup2[(2 * h + gi) % 2]
        for t0 in range(0, n_st, 2):
            bk = (t0 // 2) % 2
            ts = [t0] if t0 + 1 >= n_st else [t0, t0 + 1]
            for tpos, t in enumerate(ts):
                for il, i in enumerate(slots):
                    if MLIST[i] > t:
                        sl = 4 * par + il
                        OP("tensor", lambda te, t=t, sl=sl, il=il, tpos=tpos, bk=bk:
                           te.transpose(
                               psb_ap[bk][:, 512 * tpos + 128 * il: 512 * tpos + 128 * (il + 1)],
                               A.bf(pstr[sl], 128 * t, 128), A.bf(identb)),
                           r=[pstr[sl], identb], w=[psb_b[bk]])
            ncols = 512 * len(ts)
            OP("vector", lambda v, t0=t0, nc_=ncols, bk=bk: v.tensor_copy(
                A.bf(pt_sup, 512 * t0, nc_), psb_ap[bk][:, 0:nc_]),
                r=[psb_b[bk]], w=[pt_sup])
    def attn_stage2b(h, gi, par):
        j = h // 4
        slots = range(4 * gi, 4 * gi + 4)
        n_st = MLIST[4 * gi + 3]
        pt_sup = pt_sup2[(2 * h + gi) % 2]
        pvb = psf_b[4]
        pva = psf_ap[4]
        for t in range(n_st):
            start_col = 0
            for il, i in enumerate(slots):
                if MLIST[i] > t:
                    start_col = 128 * il
                    break
            OP("tensor", lambda te, t=t, j=j, sc=start_col, pva=pva: te.matmul(
                pva[:, sc:512], A.bf(v_tm, 512 * t + 128 * j, 128),
                A.bf(pt_sup, 512 * t + sc, 512 - sc),
                start=(t == 0), stop=(t == n_st - 1)),
                r=[v_tm, pt_sup], w=[pvb])
        OP("vector", lambda v, h=h, gi=gi, pva=pva: v.tensor_copy(
            A.bf(attnout, 1024 * h + 512 * gi, 512), pva[:, 0:512]),
            r=[pvb], w=[attnout])

    slot_ci = {}
    kc_ctr = [0]
    units = [(h, gi) for h in range(NH) for gi in range(2)]
    nu = len(units)
    for ui in range(nu + 2):
        if ui < nu:
            attn_stage1a(*units[ui], ui % 2)
        if 1 <= ui <= nu:
            attn_stage2a(*units[ui - 1], (ui - 1) % 2)
        if ui < nu:
            attn_stage1b(*units[ui], ui % 2)
        if 2 <= ui <= nu + 1:
            attn_stage2b(*units[ui - 2], (ui - 2) % 2)

    for b in pstr + q2h + diag:
        A.free(b)
    A.free(onescol)
    for b in pt_sup2:
        A.free(b)
    A.free(q_fm)
    A.free(k_fm)
    A.free(v_tm)
    A.free(qmask)

    wob.append(A.alloc("wob1", 4096))
    x1 = [A.alloc(f"x1_{tt}", 2048) for tt in range(8)]
    xs = [A.alloc(f"xs{i}", 512) for i in range(4)]

    for fs in range(4):
        if fs < 3:
            nxt = wob[(fs + 1) % 2]
            OP("gpsimd", lambda g, fs=fs, nxt=nxt: g.dma_start(
                out=A.bf(nxt), in_=wo_h[fs + 1, :, :]),
                w=[nxt], dma=f"dwo{(fs + 1) % 2}")
        wb = wob[fs % 2]
        for tt in range(8):
            sb = xs[tt % 4]
            OP("sync", lambda e, fs=fs, tt=tt, sb=sb: e.dma_start(
                out=A.f32(sb), in_=x_own[128 * tt:128 * (tt + 1), 512 * fs:512 * (fs + 1)]),
                w=[sb], dma=f"dxs{tt % 4}")
            pb = psf_b[tt % 4]
            pa = psf_ap[tt % 4]
            for d in range(KC):
                OP("tensor", lambda t, d=d, tt=tt, wb=wb, pa=pa: t.matmul(
                    pa[:, 0:512], A.bf(attnout, 1024 * d + 128 * tt, 128),
                    A.bf(wb, 512 * d, 512), start=(d == 0), stop=(d == KC - 1)),
                    r=[attnout, wb], w=[pb])
            OP("vector", lambda v, fs=fs, tt=tt, sb=sb, pa=pa: v.tensor_tensor(
                A.f32(x1[tt], 512 * fs, 512), pa[:, 0:512], A.f32(sb), ALU.add),
                r=[pb, sb], w=[x1[tt]])
    A.free(attnout)
    for b in wob + xs:
        A.free(b)

    ln2tm = [A.alloc(f"ln2tm{i}", 1024) for i in range(4)]
    ln2fm = A.alloc("ln2fm", 8192)
    for tg in range(2):
        for tl in range(4):
            tt = 4 * tg + tl
            emit_ln(lambda tt=tt: A.f32(x1[tt]), x1[tt],
                    lambda tl=tl: A.bf(ln2tm[tl]), ln2tm[tl], tt % 2)
        emit_tm_to_fm(ln2tm, lambda tl: A.bf(ln2tm[tl]), ln2fm,
                      lambda k, tg=tg: A.bf(ln2fm, 1024 * k + 512 * tg, 512), 4)
    for b in ln2tm:
        A.free(b)

    dsb = A.alloc("dsb", 2048)
    dsrow = A.alloc("dsrow", 2048)
    OP("gpsimd", lambda g: g.dma_start(
        out=A.f32(dsrow)[0:1, :], in_=ds_h[None, :]), w=[dsrow], dma="ddw0")
    for q in range(4):
        OP("tensor", lambda t, q=q: t.matmul(
            psf_ap[2 + (q % 2)][:, 0:512], A.f32(onesf)[0:1, :],
            A.f32(dsrow)[0:1, 512 * q:512 * (q + 1)], start=True, stop=True),
            r=[onesf, dsrow], w=[psf_b[2 + (q % 2)]])
        OP("vector", lambda v, q=q: v.tensor_copy(
            A.f32(dsb, 512 * q, 512), psf_ap[2 + (q % 2)][:, 0:512]),
            r=[psf_b[2 + (q % 2)]], w=[dsb])
    A.free(dsrow)
    hdn = A.alloc("hdn", 7680)
    gwb = [A.alloc(f"gwb{i}", 1024) for i in range(3)]
    upb = [A.alloc(f"upb{i}", 1024) for i in range(3)]
    dwb = [A.alloc(f"dwb{i}", 256) for i in range(4)]
    silu = [A.alloc(f"silu{i}", 512) for i in range(2)]

    thirds = [(0, 15), (15, 30), (30, 44)]
    for c0, c1 in thirds:
        for c in range(c0, c1):
            cl = c - c0
            gb, ub = gwb[c % 3], upb[c % 3]
            OP("gpsimd", lambda g, c=c, gb=gb: g.dma_start(
                out=A.bf(gb), in_=gwT_h[c, :, :]), w=[gb], dma=f"dgw{c % 3}")
            OP("gpsimd", lambda g, c=c, ub=ub: g.dma_start(
                out=A.bf(ub), in_=upT_h[c, :, :]), w=[ub], dma=f"dup{c % 3}")
            for tg in range(2):
                pg = psf_b[(2 * tg) % 4]
                pga = psf_ap[(2 * tg) % 4]
                pu = psf_b[(2 * tg + 1) % 4]
                pua = psf_ap[(2 * tg + 1) % 4]
                for k in range(KC):
                    OP("tensor", lambda t, k=k, tg=tg, gb=gb, pga=pga: t.matmul(
                        pga[:, 0:512], A.bf(gb, 128 * k, 128),
                        A.bf(ln2fm, 1024 * k + 512 * tg, 512),
                        start=(k == 0), stop=(k == KC - 1)), r=[gb, ln2fm], w=[pg])
                for k in range(KC):
                    OP("tensor", lambda t, k=k, tg=tg, ub=ub, pua=pua: t.matmul(
                        pua[:, 0:512], A.bf(ub, 128 * k, 128),
                        A.bf(ln2fm, 1024 * k + 512 * tg, 512),
                        start=(k == 0), stop=(k == KC - 1)), r=[ub, ln2fm], w=[pu])
                sl = silu[tg % 2]
                OP("scalar", lambda s, c=c, sl=sl, pga=pga: s.activation(
                    A.f32(sl), pga[:, 0:512], AF.Silu, bias=A.f32(zcol, 0, 1),
                    scale=A.f32(gs_sb, c, 1)), r=[pg, gs_sb, zcol], w=[sl])
                OP("vector", lambda v, c=c, cl=cl, tg=tg, sl=sl, pua=pua:
                   v.scalar_tensor_tensor(
                       A.bf(hdn, 1024 * cl + 512 * tg, 512), pua[:, 0:512],
                       A.f32(us_sb, c, 1), A.f32(sl), ALU.mult, ALU.mult),
                   r=[pu, us_sb, sl], w=[hdn])
        nch = c1 - c0
        for hs in range(4):
            for c in range(c0, c1):
                cl = c - c0
                db = dwb[c % 4]
                OP("gpsimd", lambda g, c=c, hs=hs, db=db: g.dma_start(
                    out=A.bf(db), in_=dwT_h[128 * c:128 * (c + 1), 512 * hs:512 * (hs + 1)]),
                    w=[db], dma=f"ddw{c % 4}")
                OP("vector", lambda v, hs=hs, db=db: v.tensor_tensor(
                    A.bf(db), A.bf(db), A.f32(dsb, 512 * hs, 512), ALU.mult),
                    r=[db, dsb], w=[db])
                for tt in range(8):
                    pd = psf_b[tt] if tt < 6 else psb_b[tt - 6]
                    pda = psf_ap[tt][:, 0:512] if tt < 6 else \
                        psb_ap[tt - 6][:, 0:1024].bitcast(F32)[:, 0:512]
                    OP("tensor", lambda t, cl=cl, tt=tt, db=db, pda=pda, nch=nch: t.matmul(
                        pda, A.bf(hdn, 1024 * cl + 128 * tt, 128), A.bf(db),
                        start=(cl == 0), stop=(cl == nch - 1)),
                        r=[hdn, db], w=[pd])
            for tt in range(8):
                pd = psf_b[tt] if tt < 6 else psb_b[tt - 6]
                pda = psf_ap[tt][:, 0:512] if tt < 6 else \
                    psb_ap[tt - 6][:, 0:1024].bitcast(F32)[:, 0:512]
                OP("vector", lambda v, hs=hs, tt=tt, pda=pda: v.tensor_tensor(
                    A.f32(x1[tt], 512 * hs, 512), pda, A.f32(x1[tt], 512 * hs, 512),
                    ALU.add), r=[pd, x1[tt]], w=[x1[tt]])

    ybufs = []
    for tt in range(8):
        yb = Buf(f"y{tt}")
        ybufs.append(yb)
        OP("sync", lambda e, tt=tt: e.dma_start(
            out=y_h[128 * tt:128 * (tt + 1), :], in_=A.f32(x1[tt])),
            r=[x1[tt]], w=[yb], dma="dout")

    for nm, shp in debug:
        pass

    OP("sync", lambda e: e.nop(), r=ybufs)

    for o in bld.ops:
        if o.dma_sem in ("dinit", "dwkv"):
            o.dma_val = 16 * bld.dma_counts[o.dma_sem]

    bld.finalize()

    with nc.Block() as block:
        for eng_name in ENGINES:
            section = getattr(block, eng_name)

            @section
            def _(eng, _e=eng_name):
                bld.emit_stream(_e, eng, sems)

    es.close()
    return nc


def _make_qmask(par):
    qm = np.zeros((8, P, 256), dtype=np.float32)
    tri = np.where(np.arange(P)[None, :] > np.arange(P)[:, None], NEG, 0.0).astype(np.float32)
    for i in range(8):
        m = MLIST[i]
        t_true = QT_PAR[par][i] + 1
        if t_true == m:
            qm[i, :, 128:256] = tri
        else:
            qm[i, :, 0:128] = tri
            qm[i, :, 128:256] = NEG
    return qm


_NC_CACHE = {}


def kernel(L0_attn_x, L0_attn_Wq, L0_attn_Wk, L0_attn_Wv, L0_attn_Wo,
           L0_ffn_gate_w, L0_ffn_up_w, L0_ffn_down_w,
           L0_ffn_gate_s, L0_ffn_up_s, L0_ffn_down_s):
    x = np.ascontiguousarray(np.asarray(L0_attn_x, dtype=np.float32))
    wq0 = np.asarray(L0_attn_Wq, dtype=np.float32)
    wk0 = np.asarray(L0_attn_Wk, dtype=np.float32)
    wv0 = np.asarray(L0_attn_Wv, dtype=np.float32)
    wo0 = np.asarray(L0_attn_Wo, dtype=np.float32)
    wq = np.ascontiguousarray(
        wq0.reshape(KC, P, KC, P).transpose(2, 1, 0, 3).reshape(KC, P, D))
    wk = np.ascontiguousarray(wk0.reshape(KC, P, 512).transpose(1, 0, 2).reshape(P, KC * 512))
    wv = np.ascontiguousarray(wv0.reshape(KC, P, 512).transpose(1, 0, 2).reshape(P, KC * 512))
    wo = np.ascontiguousarray(
        wo0.reshape(KC, P, 4, 512).transpose(2, 1, 0, 3).reshape(4, P, KC * 512))
    gwT0 = np.asarray(L0_ffn_gate_w, dtype=np.float32).T
    upT0 = np.asarray(L0_ffn_up_w, dtype=np.float32).T
    gwT = np.ascontiguousarray(
        gwT0.reshape(KC, P, NI, P).transpose(2, 1, 0, 3).reshape(NI, P, KC * P))
    upT = np.ascontiguousarray(
        upT0.reshape(KC, P, NI, P).transpose(2, 1, 0, 3).reshape(NI, P, KC * P))
    dwT = np.ascontiguousarray(np.asarray(L0_ffn_down_w, dtype=np.float32).T)
    gs = np.ascontiguousarray(np.asarray(L0_ffn_gate_s, dtype=np.float32))
    us = np.ascontiguousarray(np.asarray(L0_ffn_up_s, dtype=np.float32))
    ds = np.ascontiguousarray(np.asarray(L0_ffn_down_s, dtype=np.float32))

    qmasks = [_make_qmask(0), _make_qmask(1)]
    row_idx = [np.concatenate([np.arange(128 * g, 128 * (g + 1)) for g in QT_PAR[p]])
               for p in range(2)]

    in_maps = []
    for c in range(8):
        b, par = c // 2, c % 2
        in_maps.append({
            "x_full": x[b],
            "x_own": np.ascontiguousarray(x[b][row_idx[par]]),
            "wq": wq, "wk": wk, "wv": wv, "wo": wo,
            "gwT": gwT, "upT": upT, "dwT": dwT,
            "gs": gs, "us": us, "ds": ds,
            "qm": qmasks[par],
        })

    nc = _NC_CACHE.get("nc")
    if nc is None:
        nc = build_program()
        _NC_CACHE["nc"] = nc

    res = run_bass_kernel_spmd(nc, in_maps, core_ids=list(range(8)))

    out = np.empty((4, S, D), dtype=np.float32)
    for c in range(8):
        b, par = c // 2, c % 2
        out[b][row_idx[par]] = res.results[c]["y"]
    return out


if __name__ == "__main__":
    rng = np.random.default_rng(0)
    ins = {
        "L0_attn_x": rng.standard_normal((4, S, D), dtype=np.float32),
        "L0_attn_Wq": rng.standard_normal((D, D), dtype=np.float32) * 0.02,
        "L0_attn_Wk": rng.standard_normal((D, 512), dtype=np.float32) * 0.02,
        "L0_attn_Wv": rng.standard_normal((D, 512), dtype=np.float32) * 0.02,
        "L0_attn_Wo": rng.standard_normal((D, D), dtype=np.float32) * 0.02,
        "L0_ffn_gate_w": rng.integers(-1, 2, (INTER, D)).astype(np.float32),
        "L0_ffn_up_w": rng.integers(-1, 2, (INTER, D)).astype(np.float32),
        "L0_ffn_down_w": rng.integers(-1, 2, (D, INTER)).astype(np.float32),
        "L0_ffn_gate_s": rng.uniform(0.01, 0.1, INTER).astype(np.float32),
        "L0_ffn_up_s": rng.uniform(0.01, 0.1, INTER).astype(np.float32),
        "L0_ffn_down_s": rng.uniform(0.01, 0.1, D).astype(np.float32),
    }
    y = kernel(**ins)
    print("ran, y:", y.shape, y.dtype, np.abs(y).mean())


# version 17
# speedup vs baseline: 1.0524x; 1.0200x over previous
import numpy as np
from contextlib import ExitStack

import concourse.bass as bass
import concourse.mybir as mybir
from concourse.bass_utils import run_bass_kernel_spmd

F32 = mybir.dt.float32
BF16 = mybir.dt.bfloat16
AF = mybir.ActivationFunctionType
ALU = mybir.AluOpType
AX = mybir.AxisListType

P = 128
D = 2048
S = 2048
TOWN = 1024
NKV = 4
NH = 16
INTER = 5632
NI = 44
KC = 16
SCALE = float(D // NH) ** -0.5
MLIST = [2, 4, 6, 8, 10, 12, 14, 16]
QT_PAR = [[0, 3, 4, 7, 8, 11, 12, 15],
          [1, 2, 5, 6, 9, 10, 13, 14]]
NEG = -1.0e30
ARENA_W = 52000

ENGINES = ("sync", "gpsimd", "vector", "scalar", "tensor")


class Buf:
    __slots__ = ("name", "off", "words", "init_deps")

    def __init__(self, name, off=-1, words=0):
        self.name = name
        self.off = off
        self.words = words
        self.init_deps = None


class Op:
    __slots__ = ("idx", "eng", "emit", "deps", "kind", "dma_sem", "dma_val",
                 "needs_inc", "inc_val", "clock")

    def __init__(self, idx, eng, emit, kind, dma_sem=None, dma_val=0):
        self.idx = idx
        self.eng = eng
        self.emit = emit
        self.deps = {}
        self.kind = kind
        self.dma_sem = dma_sem
        self.dma_val = dma_val
        self.needs_inc = False
        self.inc_val = 0
        self.clock = None


class Builder:
    def __init__(self):
        self.ops = []
        self.state = {}
        self.dma_counts = {}

    def _st(self, b):
        s = self.state.get(id(b))
        if s is None:
            s = [None, [], b]
            self.state[id(b)] = s
        return s

    def op(self, eng, emit, r=(), w=(), dma=None):
        idx = len(self.ops)
        if dma is not None:
            cnt = self.dma_counts.get(dma, 0) + 1
            self.dma_counts[dma] = cnt
            o = Op(idx, eng, emit, "d", dma, 16 * cnt)
        else:
            o = Op(idx, eng, emit, "c")
        deps = o.deps

        def add(d, typ):
            if d is None:
                return
            old = deps.get(d)
            if old is None or (old != "raw" and typ == "raw"):
                deps[d] = typ

        for b in r:
            st = self._st(b)
            if b.init_deps:
                for d in b.init_deps:
                    add(d, "raw")
                b.init_deps = None
            add(st[0], "raw")
            st[1].append(idx)
        for b in w:
            st = self._st(b)
            if b.init_deps:
                for d in b.init_deps:
                    add(d, "waw")
                b.init_deps = None
            add(st[0], "waw")
            for d in st[1]:
                add(d, "war")
            st[0] = idx
            st[1] = []
        self.ops.append(o)
        return idx

    def events_of(self, b):
        s = self.state.get(id(b))
        if s is None:
            return []
        ev = list(s[1])
        if s[0] is not None:
            ev.append(s[0])
        return ev

    def finalize(self):
        ops = self.ops
        for x in ops:
            kept = {}
            for d, typ in x.deps.items():
                o = ops[d]
                if o.kind == "c" and x.kind == "c" and o.eng == x.eng:
                    if o.eng == "tensor":
                        continue
                    if typ != "raw":
                        continue
                kept[d] = typ
                if o.kind == "c":
                    o.needs_inc = True
            x.deps = kept
        cnt = {e: 0 for e in ENGINES}
        for o in ops:
            if o.kind == "c":
                if o.needs_inc:
                    cnt[o.eng] += 1
                o.inc_val = cnt[o.eng]

    def emit_stream(self, eng_name, eng, sems):
        know = {}
        own = "E_" + eng_name
        for o in self.ops:
            if o.eng != eng_name:
                continue
            need = {}
            for d in o.deps:
                p = self.ops[d]
                if p.kind == "d":
                    sem, val = p.dma_sem, p.dma_val
                else:
                    sem, val = "E_" + p.eng, p.inc_val
                    if sem == own and p.eng == eng_name and False:
                        pass
                if know.get(sem, 0) < val:
                    if need.get(sem, 0) < val:
                        need[sem] = val
                pc = p.clock
                if pc:
                    for s2, v2 in pc.items():
                        if know.get(s2, 0) < v2:
                            know[s2] = v2
                if know.get(sem, 0) < val:
                    know[sem] = val
            for sem, val in need.items():
                eng.wait_ge(sems[sem], val)
            inst = o.emit(eng)
            if o.kind == "d":
                inst.then_inc(sems[o.dma_sem], 16)
                o.clock = dict(know)
            elif o.needs_inc:
                inst.then_inc(sems[own], 1)
                o.clock = dict(know)


class Arena:
    def __init__(self, ap, words, bld):
        self.ap = ap
        self.bld = bld
        self.blocks = [(0, words, ())]

    def alloc(self, name, words):
        words = (words + 1) & ~1
        for i, (st, sz, gh) in enumerate(self.blocks):
            if sz >= words:
                b = Buf(name, st, words)
                b.init_deps = list(gh) if gh else None
                if sz == words:
                    self.blocks.pop(i)
                else:
                    self.blocks[i] = (st + words, sz - words, gh)
                return b
        raise RuntimeError(f"arena OOM for {name} ({words} words); free={self.blocks}")

    def f32(self, b, c0=0, cn=None):
        cn = b.words if cn is None else cn
        return self.ap[:, b.off + c0 : b.off + c0 + cn]

    def bf(self, b, c0=0, cn=None):
        full = self.ap[:, b.off : b.off + b.words].bitcast(BF16)
        cn = 2 * b.words if cn is None else cn
        return full[:, c0 : c0 + cn]

    def free(self, b):
        ev = tuple(self.bld.events_of(b))
        blocks = self.blocks
        st, sz = b.off, b.words
        pos = 0
        while pos < len(blocks) and blocks[pos][0] < st:
            pos += 1
        blocks.insert(pos, (st, sz, ev))
        merged = []
        for blk in blocks:
            if merged and merged[-1][0] + merged[-1][1] == blk[0]:
                a = merged[-1]
                merged[-1] = (a[0], a[1] + blk[1], tuple(set(a[2]) | set(blk[2])))
            else:
                merged.append(blk)
        self.blocks = merged


def build_program(debug=()):
    nc = bass.Bass("TRN2", target_bir_lowering=False)

    x_full = nc.dram_tensor("x_full", [S, D], F32, kind="ExternalInput").ap()
    x_own = nc.dram_tensor("x_own", [TOWN, D], F32, kind="ExternalInput").ap()
    wq_h = nc.dram_tensor("wq", [KC, P, D], F32, kind="ExternalInput").ap()
    wk_h = nc.dram_tensor("wk", [P, KC * 512], F32, kind="ExternalInput").ap()
    wv_h = nc.dram_tensor("wv", [P, KC * 512], F32, kind="ExternalInput").ap()
    wo_h = nc.dram_tensor("wo", [4, P, KC * 512], F32, kind="ExternalInput").ap()
    gwT_h = nc.dram_tensor("gwT", [NI, P, KC * 128], F32, kind="ExternalInput").ap()
    upT_h = nc.dram_tensor("upT", [NI, P, KC * 128], F32, kind="ExternalInput").ap()
    dwT_h = nc.dram_tensor("dwT", [INTER, D], F32, kind="ExternalInput").ap()
    gs_h = nc.dram_tensor("gs", [INTER], F32, kind="ExternalInput").ap()
    us_h = nc.dram_tensor("us", [INTER], F32, kind="ExternalInput").ap()
    ds_h = nc.dram_tensor("ds", [D], F32, kind="ExternalInput").ap()
    qm_h = nc.dram_tensor("qm", [8, P, 256], F32, kind="ExternalInput").ap()
    y_h = nc.dram_tensor("y", [TOWN, D], F32, kind="ExternalOutput").ap()

    dbg_h = {}
    for nm, shp in debug:
        dbg_h[nm] = nc.dram_tensor("dbg_" + nm, list(shp), F32, kind="ExternalOutput").ap()

    bld = Builder()
    es = ExitStack()

    arena_t = es.enter_context(nc.sbuf_tensor("arena", [P, ARENA_W], F32))
    A = Arena(arena_t.ap(), ARENA_W, bld)

    psfp = [es.enter_context(nc.psum_tensor(f"psfp{i}", [P, 1024], F32)) for i in range(2)]
    psf45 = [es.enter_context(nc.psum_tensor(f"psf{i}", [P, 512], F32)) for i in range(4, 6)]
    psb = [es.enter_context(nc.psum_tensor(f"psb{i}", [P, 1024], BF16)) for i in range(2)]
    psf_b = [Buf(f"psf{i}") for i in range(6)]
    psb_b = [Buf(f"psb{i}") for i in range(2)]
    psfp_ap = [t.ap() for t in psfp]
    psf_ap = [psfp_ap[0][:, 0:512], psfp_ap[0][:, 512:1024],
              psfp_ap[1][:, 0:512], psfp_ap[1][:, 512:1024],
              psf45[0].ap(), psf45[1].ap()]
    psb_ap = [t.ap() for t in psb]

    sem_names = (["E_gpsimd", "E_vector", "E_scalar", "E_tensor", "E_sync"]
                 + ["dinit", "dwkv", "dxf0", "dxf1", "dwq0", "dwq1",
                    "dwo0", "dwo1", "dgw0", "dgw1", "dgw2", "dup0", "dup1", "dup2",
                    "ddw0", "ddw1", "ddw2", "ddw3", "dxs0", "dxs1", "dxs2", "dxs3", "dout"])
    sems = {nm: es.enter_context(nc.semaphore(nm)) for nm in sem_names}

    identb = A.alloc("identb", 64)
    identf = A.alloc("identf", 128)
    onesf = A.alloc("onesf", 128)
    zcol = A.alloc("zcol", 2)
    epscol = A.alloc("epscol", 2)
    gs_c = A.alloc("gs_c", 128)
    gs_sb = A.alloc("gs_sb", 44)
    us_c = A.alloc("us_c", 128)
    us_sb = A.alloc("us_sb", 44)
    qmask = A.alloc("qmask", 2048)
    stv = [A.alloc(f"st{i}", 24) for i in range(2)]
    mvv = [A.alloc(f"mv{i}", 2) for i in range(2)]
    lnvv = [A.alloc(f"lnv{i}", 2) for i in range(2)]
    rstdv = [A.alloc(f"rstd{i}", 2) for i in range(2)]
    negm = [A.alloc(f"negm{i}", 2) for i in range(8)]
    ssum = [A.alloc(f"ssum{i}", 2) for i in range(8)]
    invs = [A.alloc(f"invs{i}", 2) for i in range(8)]

    OP = bld.op

    OP("gpsimd", lambda g: g.memset(A.bf(identb), 0.0), w=[identb])
    OP("gpsimd", lambda g: g.affine_select(
        out=A.bf(identb), in_=A.bf(identb), compare_op=ALU.not_equal,
        fill=1.0, base=0, pattern=[[-1, P]], channel_multiplier=1),
        r=[identb], w=[identb])
    OP("gpsimd", lambda g: g.memset(A.f32(identf), 0.0), w=[identf])
    OP("gpsimd", lambda g: g.affine_select(
        out=A.f32(identf), in_=A.f32(identf), compare_op=ALU.not_equal,
        fill=1.0, base=0, pattern=[[-1, P]], channel_multiplier=1),
        r=[identf], w=[identf])
    OP("vector", lambda v: v.memset(A.f32(onesf)[0:1, :], 1.0), w=[onesf])
    OP("vector", lambda v: v.memset(A.f32(zcol, 0, 1), 0.0), w=[zcol])
    OP("vector", lambda v: v.memset(A.f32(epscol, 0, 1), 1.0e-5), w=[epscol])

    OP("gpsimd", lambda g: g.dma_start(
        out=A.f32(gs_c)[0:44, :],
        in_=gs_h.rearrange("(c p) -> c p", p=P)), w=[gs_c], dma="dinit")
    OP("gpsimd", lambda g: g.dma_start(
        out=A.f32(us_c)[0:44, :],
        in_=us_h.rearrange("(c p) -> c p", p=P)), w=[us_c], dma="dinit")
    OP("gpsimd", lambda g: g.dma_start(
        out=A.f32(qmask).rearrange("p (s f) -> p s f", f=256),
        in_=qm_h.rearrange("s p f -> p s f")), w=[qmask], dma="dinit")

    OP("tensor", lambda t: t.transpose(psf_ap[0][:, 0:44], A.f32(gs_c)[0:44, :],
                                       A.f32(identf)[0:44, 0:44]),
       r=[gs_c, identf], w=[psf_b[0]])
    OP("vector", lambda v: v.tensor_copy(A.f32(gs_sb), psf_ap[0][:, 0:44]),
       r=[psf_b[0]], w=[gs_sb])
    OP("tensor", lambda t: t.transpose(psf_ap[1][:, 0:44], A.f32(us_c)[0:44, :],
                                       A.f32(identf)[0:44, 0:44]),
       r=[us_c, identf], w=[psf_b[1]])
    OP("vector", lambda v: v.tensor_copy(A.f32(us_sb), psf_ap[1][:, 0:44]),
       r=[psf_b[1]], w=[us_sb])

    A.free(gs_c)
    A.free(us_c)

    def emit_ln(x_ap_fn, xbuf, out_tm_fn, outbuf, par):
        st, mv, lnv, rstd = stv[par], mvv[par], lnvv[par], rstdv[par]
        for i in range(4):
            OP("vector", lambda v, i=i: v.bn_stats(
                A.f32(st, 6 * i, 6), x_ap_fn()[:, 512 * i:512 * (i + 1)]),
                r=[xbuf], w=[st])
        OP("vector", lambda v: v.bn_aggr(A.f32(mv, 0, 2), A.f32(st, 0, 24)),
           r=[st], w=[mv])
        OP("scalar", lambda s: s.activation(
            A.f32(lnv, 0, 1), A.f32(mv, 1, 1), AF.Ln,
            bias=A.f32(epscol, 0, 1), scale=1.0), r=[mv, epscol], w=[lnv])
        OP("scalar", lambda s: s.activation(
            A.f32(rstd, 0, 1), A.f32(lnv, 0, 1), AF.Exp,
            bias=A.f32(zcol, 0, 1), scale=-0.5), r=[lnv, zcol], w=[rstd])
        OP("vector", lambda v: v.tensor_scalar(
            out_tm_fn(), x_ap_fn(), A.f32(mv, 0, 1), A.f32(rstd, 0, 1),
            ALU.subtract, ALU.mult), r=[xbuf, mv, rstd], w=[outbuf])

    def emit_tm_to_fm(tm_bufs, tm_fn, dst_buf, dst_fn, ntile, bank_off=0):
        for k in range(KC):
            bk = (k + bank_off) % 2
            for tt in range(ntile):
                OP("tensor", lambda t, k=k, tt=tt, bk=bk: t.transpose(
                    psb_ap[bk][:, 128 * tt:128 * (tt + 1)],
                    tm_fn(tt)[:, 128 * k:128 * (k + 1)], A.bf(identb)),
                    r=[tm_bufs[tt], identb], w=[psb_b[bk]])
            eng = "vector" if (k % 2 == 0) else "scalar"
            if eng == "vector":
                OP("vector", lambda v, k=k, bk=bk: v.tensor_copy(
                    dst_fn(k), psb_ap[bk][:, 0:128 * ntile]),
                    r=[psb_b[bk]], w=[dst_buf])
            else:
                OP("scalar", lambda s, k=k, bk=bk: s.copy(
                    dst_fn(k), psb_ap[bk][:, 0:128 * ntile]),
                    r=[psb_b[bk]], w=[dst_buf])

    xt = [A.alloc(f"xt{i}", 2048) for i in range(2)]
    ln1tm = [A.alloc(f"ln1tm{i}", 1024) for i in range(4)]
    ln1own = A.alloc("ln1own", 8192)
    q_fm = A.alloc("q_fm", 8192)
    k_fm = A.alloc("k_fm", 4096)
    v_tm = A.alloc("v_tm", 4096)

    wqb = [A.alloc(f"wqb{i}", 1024) for i in range(2)]
    for tg in range(2):
        for tl in range(4):
            tt = 4 * tg + tl
            xb = xt[tt % 2]
            OP("sync", lambda e, tt=tt, xb=xb: e.dma_start(
                out=A.f32(xb), in_=x_own[128 * tt:128 * (tt + 1), :]),
                w=[xb], dma=f"dxf{tt % 2}")
            emit_ln(lambda xb=xb: A.f32(xb), xb,
                    lambda tl=tl: A.bf(ln1tm[tl]), ln1tm[tl], tt % 2)
        emit_tm_to_fm(ln1tm, lambda tl: A.bf(ln1tm[tl]), ln1own,
                      lambda k, tg=tg: A.bf(ln1own, 1024 * k + 512 * tg, 512), 4)

    for f in range(KC):
        wb = wqb[f % 2]
        OP("gpsimd", lambda g, f=f, wb=wb: g.dma_start(
            out=A.bf(wb), in_=wq_h[f, :, :]), w=[wb], dma=f"dwq{f % 2}")
        for tg in range(2):
            pb = psf_b[(2 * f + tg) % 4]
            pa = psf_ap[(2 * f + tg) % 4]
            for k in range(KC):
                OP("tensor", lambda t, k=k, f=f, tg=tg, wb=wb, pa=pa: t.matmul(
                    pa[:, 0:512], A.bf(wb, 128 * k, 128),
                    A.bf(ln1own, 1024 * k + 512 * tg, 512),
                    start=(k == 0), stop=(k == KC - 1)),
                    r=[wb, ln1own], w=[pb])
            eng = "vector" if tg == 0 else "scalar"
            dst = lambda f=f, tg=tg: A.bf(q_fm, 1024 * f + 512 * tg, 512)
            if eng == "vector":
                OP("vector", lambda v, pa=pa, dst=dst, pb=pb: v.tensor_copy(dst(), pa[:, 0:512]),
                   r=[pb], w=[q_fm])
            else:
                OP("scalar", lambda sx, pa=pa, dst=dst, pb=pb: sx.copy(dst(), pa[:, 0:512]),
                   r=[pb], w=[q_fm])

    A.free(ln1own)
    for b in wqb:
        A.free(b)

    wkb = A.alloc("wkb", 4096)
    wvb = A.alloc("wvb", 4096)
    OP("gpsimd", lambda g: g.dma_start(out=A.bf(wkb), in_=wk_h[:, :]),
       w=[wkb], dma="dwkv")
    OP("gpsimd", lambda g: g.dma_start(out=A.bf(wvb), in_=wv_h[:, :]),
       w=[wvb], dma="dwkv")

    ln1grp = A.alloc("ln1grp", 4096)
    for tg in range(4):
        for tl in range(4):
            tt = 4 * tg + tl
            xb = xt[tt % 2]
            OP("sync", lambda e, tt=tt, xb=xb: e.dma_start(
                out=A.f32(xb), in_=x_full[128 * tt:128 * (tt + 1), :]),
                w=[xb], dma=f"dxf{tt % 2}")
            emit_ln(lambda xb=xb: A.f32(xb), xb,
                    lambda tl=tl: A.bf(ln1tm[tl]), ln1tm[tl], tt % 2)
        emit_tm_to_fm(ln1tm, lambda tl: A.bf(ln1tm[tl]), ln1grp,
                      lambda k: A.bf(ln1grp, 512 * k, 512), 4)
        for j in range(NKV):
            pb = psf_b[j % 2]
            pa = psf_ap[j % 2]
            for k in range(KC):
                OP("tensor", lambda t, j=j, k=k, pa=pa: t.matmul(
                    pa[:, 0:512], A.bf(wkb, 512 * k + 128 * j, 128),
                    A.bf(ln1grp, 512 * k, 512),
                    start=(k == 0), stop=(k == KC - 1)), r=[wkb, ln1grp], w=[pb])
            eng = "vector" if j % 2 == 0 else "scalar"
            dst = lambda j=j, tg=tg: A.bf(k_fm, 2048 * j + 512 * tg, 512)
            if eng == "vector":
                OP("vector", lambda v, pa=pa, dst=dst: v.tensor_copy(dst(), pa[:, 0:512]),
                   r=[pb], w=[k_fm])
            else:
                OP("scalar", lambda s, pa=pa, dst=dst: s.copy(dst(), pa[:, 0:512]),
                   r=[pb], w=[k_fm])
        for sl in range(4):
            st_g = 4 * tg + sl
            pb = psf_b[2 + sl % 2]
            pa = psf_ap[2 + sl % 2]
            for k in range(KC):
                OP("tensor", lambda t, sl=sl, k=k, pa=pa: t.matmul(
                    pa[:, 0:512], A.bf(ln1grp, 512 * k + 128 * sl, 128),
                    A.bf(wvb, 512 * k, 512),
                    start=(k == 0), stop=(k == KC - 1)), r=[wvb, ln1grp], w=[pb])
            eng = "vector" if sl % 2 == 0 else "scalar"
            dst = lambda st_g=st_g: A.bf(v_tm, 512 * st_g, 512)
            if eng == "vector":
                OP("vector", lambda v, pa=pa, dst=dst: v.tensor_copy(dst(), pa[:, 0:512]),
                   r=[pb], w=[v_tm])
            else:
                OP("scalar", lambda s, pa=pa, dst=dst: s.copy(dst(), pa[:, 0:512]),
                   r=[pb], w=[v_tm])

    for b in xt + ln1tm + [ln1grp, wkb, wvb]:
        A.free(b)

    pt_sup2 = [A.alloc(f"pt_sup{i}", 4096) for i in range(2)]
    attnout = A.alloc("attnout", 8192)
    wob = [A.alloc("wob0", 4096)]
    OP("gpsimd", lambda g: g.dma_start(out=A.bf(wob[0]), in_=wo_h[0, :, :]),
       w=[wob[0]], dma="dwo0")

    onescol = A.alloc("onescol", 2)
    k2scr = [A.alloc(f"k2scr{i}", 512) for i in range(2)]
    kn2row = A.alloc("kn2row", 2048)
    kn2m4 = A.alloc("kn2m4", 4)
    OP("vector", lambda v: v.memset(A.f32(onescol, 0, 1), 1.0), w=[onescol])
    for j in range(NKV):
        for ch in range(4):
            k2 = k2scr[ch % 2]
            OP("gpsimd", lambda g, j=j, ch=ch, k2=k2: g.tensor_tensor(
                A.f32(k2), A.bf(k_fm, 2048 * j + 512 * ch, 512),
                A.bf(k_fm, 2048 * j + 512 * ch, 512), ALU.mult),
                r=[k_fm], w=[k2])
            OP("tensor", lambda t, k2=k2: t.matmul(
                psf_ap[4][0:1, 0:512], A.f32(onescol, 0, 1), A.f32(k2),
                start=True, stop=True), r=[onescol, k2], w=[psf_b[4]])
            OP("vector", lambda v, ch=ch: v.tensor_copy(
                A.f32(kn2row, 512 * ch, 512)[0:1, :], psf_ap[4][0:1, 0:512]),
                r=[psf_b[4]], w=[kn2row])
        OP("vector", lambda v, j=j: v.tensor_reduce(
            A.f32(kn2m4, j, 1)[0:1, :], A.f32(kn2row)[0:1, :], AX.X, ALU.max),
            r=[kn2row], w=[kn2m4])
    A.free(kn2row)
    for b in k2scr:
        A.free(b)

    q2h = [A.alloc("q2h0", 512)]
    negbh = [A.alloc(f"negbh{i}", 2) for i in range(NH)]
    qm2 = A.alloc("qm2", 4)
    onescol_b = A.alloc("onescol_b", 2)
    diag = [A.alloc(f"diag{i}", 64) for i in range(8)]
    ssc = [A.alloc(f"ssc{i}", 8) for i in range(8)]
    pstr = [A.alloc(f"p{i}", 64 * MLIST[i]) for i in range(8)]

    OP("vector", lambda v: v.memset(A.bf(onescol_b, 0, 1), 1.0), w=[onescol_b])
    for h0 in range(NH):
        j0 = h0 // 4
        q2 = q2h[0]
        OP("gpsimd", lambda g, h0=h0, q2=q2: g.tensor_tensor(
            A.bf(q2), A.bf(q_fm, 1024 * h0, 1024), A.bf(q_fm, 1024 * h0, 1024),
            ALU.mult), r=[q_fm], w=[q2])
        for c2 in range(2):
            OP("tensor", lambda t, q2=q2, c2=c2: t.matmul(
                psf_ap[5][0:1, 0:512], A.bf(onescol_b, 0, 1), A.bf(q2, 512 * c2, 512),
                start=True, stop=True), r=[onescol_b, q2], w=[psf_b[5]])
            OP("vector", lambda v, h0=h0, c2=c2: v.tensor_reduce(
                A.f32(qm2, c2, 1)[0:1, :], psf_ap[5][0:1, 0:512], AX.X, ALU.max),
                r=[psf_b[5]], w=[qm2])
        OP("vector", lambda v: v.tensor_reduce(
            A.f32(qm2, 2, 1)[0:1, :], A.f32(qm2, 0, 2)[0:1, :], AX.X, ALU.max),
            r=[qm2], w=[qm2])
        OP("vector", lambda v, j0=j0: v.tensor_tensor(
            A.f32(qm2, 3, 1)[0:1, :], A.f32(qm2, 2, 1)[0:1, :],
            A.f32(kn2m4, j0, 1)[0:1, :], ALU.add), r=[qm2, kn2m4], w=[qm2])
        OP("tensor", lambda t: t.matmul(
            psf_ap[5][:, 0:1], A.f32(onesf)[0:1, :], A.f32(qm2, 3, 1)[0:1, :],
            start=True, stop=True), r=[onesf, qm2], w=[psf_b[5]])
        OP("vector", lambda v, h0=h0: v.tensor_scalar(
            A.f32(negbh[h0], 0, 1), psf_ap[5][:, 0:1], -SCALE / 2.0, None,
            ALU.mult), r=[psf_b[5]], w=[negbh[h0]])

    def attn_stage1a(h, gi, par):
        j = h // 4
        ui = 2 * h + gi
        slots = range(4 * gi, 4 * gi + 4)
        for il, i in enumerate(slots):
            sl = 4 * par + il
            m = MLIST[i]
            L = 128 * m
            mstart = L - 256
            pbuf = pstr[sl]
            nbu, nbc = negbh[h], 0
            ci = 0
            for sc0 in range(0, L, 1024):
                pw = min(1024, L - sc0)
                pp = kc_ctr[0] % 2
                kc_ctr[0] += 1
                for sc in range(sc0, sc0 + pw, 512):
                    w_ = min(512, L - sc)
                    pb = psf_b[2 * pp + ((sc - sc0) // 512)]
                    pa = psf_ap[2 * pp + ((sc - sc0) // 512)]
                    OP("tensor", lambda t, h=h, i=i, j=j, sc=sc, w_=w_, pa=pa:
                       t.matmul(
                           pa[:, 0:w_], A.bf(q_fm, 1024 * h + 128 * i, 128),
                           A.bf(k_fm, 2048 * j + sc, w_), start=True, stop=True),
                        r=[q_fm, k_fm], w=[pb])
                if sc0 + pw == L:
                    mo_ = mstart - sc0
                    mb = [psf_b[2 * pp + (mo_ // 512)], psf_b[2 * pp + ((mo_ + 255) // 512)]]
                    OP("vector", lambda v, i=i, pp=pp, mo=mstart - sc0:
                       v.scalar_tensor_tensor(
                           psfp_ap[pp][:, mo:mo + 256], psfp_ap[pp][:, mo:mo + 256],
                           1.0, A.f32(qmask, 256 * i, 256), ALU.mult, ALU.add),
                        r=[qmask] + mb, w=mb)
                rbs = ([psf_b[2 * pp], psf_b[2 * pp + 1]] if pw > 512
                       else [psf_b[2 * pp]])
                OP("scalar", lambda s, sc0=sc0, pw=pw, pp=pp, nbu=nbu, nbc=nbc,
                   sl=sl, ci=ci, pbuf=pbuf: s.activation(
                       A.bf(pbuf, sc0, pw), psfp_ap[pp][:, 0:pw], AF.Exp,
                       bias=A.f32(nbu, nbc, 1), scale=SCALE,
                       accum_out=A.f32(ssc[sl], ci, 1)),
                    r=rbs + [nbu], w=[pbuf, ssc[sl]])
                ci += 1
            slot_ci[sl] = ci

    def attn_stage1b(h, gi, par):
        slots = range(4 * gi, 4 * gi + 4)
        for il, i in enumerate(slots):
            sl = 4 * par + il
            L = 128 * MLIST[i]
            pbuf = pstr[sl]
            ci = slot_ci[sl]
            OP("vector", lambda v, sl=sl, ci=ci: v.tensor_reduce(
                A.f32(ssum[sl], 0, 1), A.f32(ssc[sl], 0, ci), AX.X, ALU.add),
                r=[ssc[sl]], w=[ssum[sl]])
            OP("vector", lambda v, sl=sl: v.reciprocal(
                A.f32(invs[sl], 0, 1), A.f32(ssum[sl], 0, 1)),
                r=[ssum[sl]], w=[invs[sl]])
            OP("vector", lambda v, L=L, sl=sl, pbuf=pbuf: v.tensor_scalar(
                A.bf(pbuf, 0, L), A.bf(pbuf, 0, L), A.f32(invs[sl], 0, 1),
                None, ALU.mult), r=[pbuf, invs[sl]], w=[pbuf])

    def attn_stage2a(h, gi, par):
        j = h // 4
        slots = range(4 * gi, 4 * gi + 4)
        n_st = MLIST[4 * gi + 3]
        pt_sup = pt_sup2[(2 * h + gi) % 2]
        for t0 in range(0, n_st, 2):
            bk = (t0 // 2) % 2
            ts = [t0] if t0 + 1 >= n_st else [t0, t0 + 1]
            for tpos, t in enumerate(ts):
                for il, i in enumerate(slots):
                    if MLIST[i] > t:
                        sl = 4 * par + il
                        OP("tensor", lambda te, t=t, sl=sl, il=il, tpos=tpos, bk=bk:
                           te.transpose(
                               psb_ap[bk][:, 512 * tpos + 128 * il: 512 * tpos + 128 * (il + 1)],
                               A.bf(pstr[sl], 128 * t, 128), A.bf(identb)),
                           r=[pstr[sl], identb], w=[psb_b[bk]])
            ncols = 512 * len(ts)
            OP("vector", lambda v, t0=t0, nc_=ncols, bk=bk: v.tensor_copy(
                A.bf(pt_sup, 512 * t0, nc_), psb_ap[bk][:, 0:nc_]),
                r=[psb_b[bk]], w=[pt_sup])
    def attn_stage2b(h, gi, par):
        j = h // 4
        slots = range(4 * gi, 4 * gi + 4)
        n_st = MLIST[4 * gi + 3]
        pt_sup = pt_sup2[(2 * h + gi) % 2]
        pvb = psf_b[4]
        pva = psf_ap[4]
        for t in range(n_st):
            start_col = 0
            for il, i in enumerate(slots):
                if MLIST[i] > t:
                    start_col = 128 * il
                    break
            OP("tensor", lambda te, t=t, j=j, sc=start_col, pva=pva: te.matmul(
                pva[:, sc:512], A.bf(v_tm, 512 * t + 128 * j, 128),
                A.bf(pt_sup, 512 * t + sc, 512 - sc),
                start=(t == 0), stop=(t == n_st - 1)),
                r=[v_tm, pt_sup], w=[pvb])
        OP("vector", lambda v, h=h, gi=gi, pva=pva: v.tensor_copy(
            A.bf(attnout, 1024 * h + 512 * gi, 512), pva[:, 0:512]),
            r=[pvb], w=[attnout])

    slot_ci = {}
    kc_ctr = [0]
    units = [(h, gi) for h in range(NH) for gi in range(2)]
    nu = len(units)
    for ui in range(nu + 2):
        if ui < nu:
            attn_stage1a(*units[ui], ui % 2)
        if 1 <= ui <= nu:
            attn_stage2a(*units[ui - 1], (ui - 1) % 2)
        if ui < nu:
            attn_stage1b(*units[ui], ui % 2)
        if 2 <= ui <= nu + 1:
            attn_stage2b(*units[ui - 2], (ui - 2) % 2)

    for b in pstr + q2h + diag:
        A.free(b)
    A.free(onescol)
    for b in pt_sup2:
        A.free(b)
    A.free(q_fm)
    A.free(k_fm)
    A.free(v_tm)
    A.free(qmask)

    wob.append(A.alloc("wob1", 4096))
    x1 = [A.alloc(f"x1_{tt}", 2048) for tt in range(8)]
    xs = [A.alloc(f"xs{i}", 512) for i in range(4)]

    for fs in range(4):
        if fs < 3:
            nxt = wob[(fs + 1) % 2]
            OP("gpsimd", lambda g, fs=fs, nxt=nxt: g.dma_start(
                out=A.bf(nxt), in_=wo_h[fs + 1, :, :]),
                w=[nxt], dma=f"dwo{(fs + 1) % 2}")
        wb = wob[fs % 2]
        for tt in range(8):
            sb = xs[tt % 4]
            OP("sync", lambda e, fs=fs, tt=tt, sb=sb: e.dma_start(
                out=A.f32(sb), in_=x_own[128 * tt:128 * (tt + 1), 512 * fs:512 * (fs + 1)]),
                w=[sb], dma=f"dxs{tt % 4}")
            pb = psf_b[tt % 4]
            pa = psf_ap[tt % 4]
            for d in range(KC):
                OP("tensor", lambda t, d=d, tt=tt, wb=wb, pa=pa: t.matmul(
                    pa[:, 0:512], A.bf(attnout, 1024 * d + 128 * tt, 128),
                    A.bf(wb, 512 * d, 512), start=(d == 0), stop=(d == KC - 1)),
                    r=[attnout, wb], w=[pb])
            OP("vector", lambda v, fs=fs, tt=tt, sb=sb, pa=pa: v.tensor_tensor(
                A.f32(x1[tt], 512 * fs, 512), pa[:, 0:512], A.f32(sb), ALU.add),
                r=[pb, sb], w=[x1[tt]])
    A.free(attnout)
    for b in wob + xs:
        A.free(b)

    ln2tm = [A.alloc(f"ln2tm{i}", 1024) for i in range(4)]
    ln2fm = A.alloc("ln2fm", 8192)
    for tg in range(2):
        for tl in range(4):
            tt = 4 * tg + tl
            emit_ln(lambda tt=tt: A.f32(x1[tt]), x1[tt],
                    lambda tl=tl: A.bf(ln2tm[tl]), ln2tm[tl], tt % 2)
        emit_tm_to_fm(ln2tm, lambda tl: A.bf(ln2tm[tl]), ln2fm,
                      lambda k, tg=tg: A.bf(ln2fm, 1024 * k + 512 * tg, 512), 4)
    for b in ln2tm:
        A.free(b)

    hdn = A.alloc("hdn", 8192)
    dsb = A.alloc("dsb", 2048)
    dsrow = A.alloc("dsrow", 2048)
    OP("gpsimd", lambda g: g.dma_start(
        out=A.f32(dsrow)[0:1, :], in_=ds_h[None, :]), w=[dsrow], dma="ddw0")
    for q in range(4):
        OP("tensor", lambda t, q=q: t.matmul(
            psf_ap[2 + (q % 2)][:, 0:512], A.f32(onesf)[0:1, :],
            A.f32(dsrow)[0:1, 512 * q:512 * (q + 1)], start=True, stop=True),
            r=[onesf, dsrow], w=[psf_b[2 + (q % 2)]])
        OP("vector", lambda v, q=q: v.tensor_copy(
            A.f32(dsb, 512 * q, 512), psf_ap[2 + (q % 2)][:, 0:512]),
            r=[psf_b[2 + (q % 2)]], w=[dsb])
    A.free(dsrow)
    gwb = [A.alloc(f"gwb{i}", 1024) for i in range(3)]
    upb = [A.alloc(f"upb{i}", 1024) for i in range(3)]
    dwb = [A.alloc(f"dwb{i}", 256) for i in range(4)]
    silu = [A.alloc(f"silu{i}", 512) for i in range(2)]

    thirds = [(0, 16), (16, 32), (32, 44)]
    for c0, c1 in thirds:
        for c in range(c0, c1):
            cl = c - c0
            gb, ub = gwb[c % 3], upb[c % 3]
            OP("gpsimd", lambda g, c=c, gb=gb: g.dma_start(
                out=A.bf(gb), in_=gwT_h[c, :, :]), w=[gb], dma=f"dgw{c % 3}")
            OP("gpsimd", lambda g, c=c, ub=ub: g.dma_start(
                out=A.bf(ub), in_=upT_h[c, :, :]), w=[ub], dma=f"dup{c % 3}")
            for tg in range(2):
                pg = psf_b[(2 * tg) % 4]
                pga = psf_ap[(2 * tg) % 4]
                pu = psf_b[(2 * tg + 1) % 4]
                pua = psf_ap[(2 * tg + 1) % 4]
                for k in range(KC):
                    OP("tensor", lambda t, k=k, tg=tg, gb=gb, pga=pga: t.matmul(
                        pga[:, 0:512], A.bf(gb, 128 * k, 128),
                        A.bf(ln2fm, 1024 * k + 512 * tg, 512),
                        start=(k == 0), stop=(k == KC - 1)), r=[gb, ln2fm], w=[pg])
                for k in range(KC):
                    OP("tensor", lambda t, k=k, tg=tg, ub=ub, pua=pua: t.matmul(
                        pua[:, 0:512], A.bf(ub, 128 * k, 128),
                        A.bf(ln2fm, 1024 * k + 512 * tg, 512),
                        start=(k == 0), stop=(k == KC - 1)), r=[ub, ln2fm], w=[pu])
                sl = silu[tg % 2]
                OP("scalar", lambda s, c=c, sl=sl, pga=pga: s.activation(
                    A.f32(sl), pga[:, 0:512], AF.Silu, bias=A.f32(zcol, 0, 1),
                    scale=A.f32(gs_sb, c, 1)), r=[pg, gs_sb, zcol], w=[sl])
                OP("vector", lambda v, c=c, cl=cl, tg=tg, sl=sl, pua=pua:
                   v.scalar_tensor_tensor(
                       A.bf(hdn, 1024 * cl + 512 * tg, 512), pua[:, 0:512],
                       A.f32(us_sb, c, 1), A.f32(sl), ALU.mult, ALU.mult),
                   r=[pu, us_sb, sl], w=[hdn])
        nch = c1 - c0
        for hs in range(4):
            for c in range(c0, c1):
                cl = c - c0
                db = dwb[c % 4]
                OP("gpsimd", lambda g, c=c, hs=hs, db=db: g.dma_start(
                    out=A.bf(db), in_=dwT_h[128 * c:128 * (c + 1), 512 * hs:512 * (hs + 1)]),
                    w=[db], dma=f"ddw{c % 4}")
                OP("vector", lambda v, hs=hs, db=db: v.tensor_tensor(
                    A.bf(db), A.bf(db), A.f32(dsb, 512 * hs, 512), ALU.mult),
                    r=[db, dsb], w=[db])
                for tt in range(8):
                    pd = psf_b[tt] if tt < 6 else psb_b[tt - 6]
                    pda = psf_ap[tt][:, 0:512] if tt < 6 else \
                        psb_ap[tt - 6][:, 0:1024].bitcast(F32)[:, 0:512]
                    OP("tensor", lambda t, cl=cl, tt=tt, db=db, pda=pda, nch=nch: t.matmul(
                        pda, A.bf(hdn, 1024 * cl + 128 * tt, 128), A.bf(db),
                        start=(cl == 0), stop=(cl == nch - 1)),
                        r=[hdn, db], w=[pd])
            for tt in range(8):
                pd = psf_b[tt] if tt < 6 else psb_b[tt - 6]
                pda = psf_ap[tt][:, 0:512] if tt < 6 else \
                    psb_ap[tt - 6][:, 0:1024].bitcast(F32)[:, 0:512]
                OP("vector", lambda v, hs=hs, tt=tt, pda=pda: v.tensor_tensor(
                    A.f32(x1[tt], 512 * hs, 512), pda, A.f32(x1[tt], 512 * hs, 512),
                    ALU.add), r=[pd, x1[tt]], w=[x1[tt]])

    ybufs = []
    for tt in range(8):
        yb = Buf(f"y{tt}")
        ybufs.append(yb)
        OP("sync", lambda e, tt=tt: e.dma_start(
            out=y_h[128 * tt:128 * (tt + 1), :], in_=A.f32(x1[tt])),
            r=[x1[tt]], w=[yb], dma="dout")

    for nm, shp in debug:
        pass

    OP("sync", lambda e: e.nop(), r=ybufs)

    for o in bld.ops:
        if o.dma_sem in ("dinit", "dwkv"):
            o.dma_val = 16 * bld.dma_counts[o.dma_sem]

    bld.finalize()

    with nc.Block() as block:
        for eng_name in ENGINES:
            section = getattr(block, eng_name)

            @section
            def _(eng, _e=eng_name):
                bld.emit_stream(_e, eng, sems)

    es.close()
    return nc


def _make_qmask(par):
    qm = np.zeros((8, P, 256), dtype=np.float32)
    tri = np.where(np.arange(P)[None, :] > np.arange(P)[:, None], NEG, 0.0).astype(np.float32)
    for i in range(8):
        m = MLIST[i]
        t_true = QT_PAR[par][i] + 1
        if t_true == m:
            qm[i, :, 128:256] = tri
        else:
            qm[i, :, 0:128] = tri
            qm[i, :, 128:256] = NEG
    return qm


_NC_CACHE = {}


def kernel(L0_attn_x, L0_attn_Wq, L0_attn_Wk, L0_attn_Wv, L0_attn_Wo,
           L0_ffn_gate_w, L0_ffn_up_w, L0_ffn_down_w,
           L0_ffn_gate_s, L0_ffn_up_s, L0_ffn_down_s):
    x = np.ascontiguousarray(np.asarray(L0_attn_x, dtype=np.float32))
    wq0 = np.asarray(L0_attn_Wq, dtype=np.float32)
    wk0 = np.asarray(L0_attn_Wk, dtype=np.float32)
    wv0 = np.asarray(L0_attn_Wv, dtype=np.float32)
    wo0 = np.asarray(L0_attn_Wo, dtype=np.float32)
    wq = np.ascontiguousarray(
        wq0.reshape(KC, P, KC, P).transpose(2, 1, 0, 3).reshape(KC, P, D))
    wk = np.ascontiguousarray(wk0.reshape(KC, P, 512).transpose(1, 0, 2).reshape(P, KC * 512))
    wv = np.ascontiguousarray(wv0.reshape(KC, P, 512).transpose(1, 0, 2).reshape(P, KC * 512))
    wo = np.ascontiguousarray(
        wo0.reshape(KC, P, 4, 512).transpose(2, 1, 0, 3).reshape(4, P, KC * 512))
    gwT0 = np.asarray(L0_ffn_gate_w, dtype=np.float32).T
    upT0 = np.asarray(L0_ffn_up_w, dtype=np.float32).T
    gwT = np.ascontiguousarray(
        gwT0.reshape(KC, P, NI, P).transpose(2, 1, 0, 3).reshape(NI, P, KC * P))
    upT = np.ascontiguousarray(
        upT0.reshape(KC, P, NI, P).transpose(2, 1, 0, 3).reshape(NI, P, KC * P))
    dwT = np.ascontiguousarray(np.asarray(L0_ffn_down_w, dtype=np.float32).T)
    gs = np.ascontiguousarray(np.asarray(L0_ffn_gate_s, dtype=np.float32))
    us = np.ascontiguousarray(np.asarray(L0_ffn_up_s, dtype=np.float32))
    ds = np.ascontiguousarray(np.asarray(L0_ffn_down_s, dtype=np.float32))

    qmasks = [_make_qmask(0), _make_qmask(1)]
    row_idx = [np.concatenate([np.arange(128 * g, 128 * (g + 1)) for g in QT_PAR[p]])
               for p in range(2)]

    in_maps = []
    for c in range(8):
        b, par = c // 2, c % 2
        in_maps.append({
            "x_full": x[b],
            "x_own": np.ascontiguousarray(x[b][row_idx[par]]),
            "wq": wq, "wk": wk, "wv": wv, "wo": wo,
            "gwT": gwT, "upT": upT, "dwT": dwT,
            "gs": gs, "us": us, "ds": ds,
            "qm": qmasks[par],
        })

    nc = _NC_CACHE.get("nc")
    if nc is None:
        nc = build_program()
        _NC_CACHE["nc"] = nc

    res = run_bass_kernel_spmd(nc, in_maps, core_ids=list(range(8)))

    out = np.empty((4, S, D), dtype=np.float32)
    for c in range(8):
        b, par = c // 2, c % 2
        out[b][row_idx[par]] = res.results[c]["y"]
    return out


if __name__ == "__main__":
    rng = np.random.default_rng(0)
    ins = {
        "L0_attn_x": rng.standard_normal((4, S, D), dtype=np.float32),
        "L0_attn_Wq": rng.standard_normal((D, D), dtype=np.float32) * 0.02,
        "L0_attn_Wk": rng.standard_normal((D, 512), dtype=np.float32) * 0.02,
        "L0_attn_Wv": rng.standard_normal((D, 512), dtype=np.float32) * 0.02,
        "L0_attn_Wo": rng.standard_normal((D, D), dtype=np.float32) * 0.02,
        "L0_ffn_gate_w": rng.integers(-1, 2, (INTER, D)).astype(np.float32),
        "L0_ffn_up_w": rng.integers(-1, 2, (INTER, D)).astype(np.float32),
        "L0_ffn_down_w": rng.integers(-1, 2, (D, INTER)).astype(np.float32),
        "L0_ffn_gate_s": rng.uniform(0.01, 0.1, INTER).astype(np.float32),
        "L0_ffn_up_s": rng.uniform(0.01, 0.1, INTER).astype(np.float32),
        "L0_ffn_down_s": rng.uniform(0.01, 0.1, D).astype(np.float32),
    }
    y = kernel(**ins)
    print("ran, y:", y.shape, y.dtype, np.abs(y).mean())
